# version 06
# baseline (speedup 1.0000x reference)
import numpy as np
import ml_dtypes
import concourse.bass as bass
import concourse.mybir as mybir

F32 = mybir.dt.float32
BF16 = mybir.dt.bfloat16
I32 = mybir.dt.int32
AF = mybir.ActivationFunctionType
ALU = mybir.AluOpType

B, S, D, NH, DH, HID = 4, 4096, 1024, 16, 64, 3072
HL = 8
T_OWN = 2048
SHIFT = 12.0
QS = 127.0

R_XLN = 0
R_QK = 8192
R_V65 = 40960
R_CS = 57600
R_W = 65792
R_WD = 78080
W16 = 102656

Q_FM = R_QK
K_FM = R_QK + 16384
P_BUF = R_CS
CTX_STG = R_CS + 4096
RECIP_PAD = R_CS + 6144
BV_PAD = R_CS + 7168
BO_PAD = R_CS
XT_STG = R_CS
JUNK_A = R_CS + 4096

SC_XT = 0
SC_T12 = 2048
SC_TAB = 8192
SC_BC = 11264


class Sched:
    ENGS = ("t", "v", "s", "g", "y")

    def __init__(self):
        self.cmds = {e: [] for e in self.ENGS}
        self.counts = {}
        self.lastwait = {}

    def do(self, eng, fn, sem=None, amt=1):
        if sem is None:
            self.cmds[eng].append(("plain", fn))
            return None
        c = self.counts.get(sem, 0) + amt
        self.counts[sem] = c
        self.cmds[eng].append(("inc", fn, sem, amt))
        return (sem, c)

    def wait(self, eng, ev):
        if ev is None:
            return
        sem, val = ev
        key = (eng, sem)
        if self.lastwait.get(key, -1) >= val:
            return
        self.lastwait[key] = val
        self.cmds[eng].append(("wait", sem, val))

    def wait_all(self, eng, evs):
        for ev in evs:
            self.wait(eng, ev)

    def frontier(self):
        return [(s, c) for s, c in self.counts.items()]

    def replay(self, eng, handle, sems):
        for cmd in self.cmds[eng]:
            if cmd[0] == "plain":
                cmd[1](handle)
            elif cmd[0] == "inc":
                cmd[1](handle).then_inc(sems[cmd[2]], cmd[3])
            else:
                handle.wait_ge(sems[cmd[1]], cmd[2])


class Psum:
    def __init__(self, sched):
        self.S = sched
        self.write_ev = [None] * 8
        self.read_ev = [None] * 8

    def guard_write(self, bank):
        self.S.wait("t", self.read_ev[bank])

    def set_write(self, bank, ev):
        self.write_ev[bank] = ev

    def guard_read(self, eng, bank):
        self.S.wait(eng, self.write_ev[bank])

    def set_read(self, bank, ev):
        self.read_ev[bank] = ev


def build(debug=(), n_pairs=4):
    nc = bass.Bass("TRN2", target_bir_lowering=False)
    dbg = set(debug)

    xseq = nc.dram_tensor("xseq", [S, D], F32, kind="ExternalInput").ap()
    xown = nc.dram_tensor("xown", [T_OWN, D], F32, kind="ExternalInput").ap()
    Wq_d = nc.dram_tensor("Wq_c", [D, 512], F32, kind="ExternalInput").ap()
    Wk_d = nc.dram_tensor("Wk_c", [D, 512], F32, kind="ExternalInput").ap()
    Wv_d = nc.dram_tensor("Wv_c", [D, 512], F32, kind="ExternalInput").ap()
    bqc_d = nc.dram_tensor("bqcol", [128, 4], F32, kind="ExternalInput").ap()
    bkc_d = nc.dram_tensor("bkcol", [128, 4], F32, kind="ExternalInput").ap()
    bv_d = nc.dram_tensor("bv_c", [1, 512], F32, kind="ExternalInput").ap()
    e0_d = nc.dram_tensor("e0", [128, 128], BF16, kind="ExternalInput").ap()
    Woa_d = nc.dram_tensor("Wo_a", [D, D], F32, kind="ExternalInput").ap()
    boa_d = nc.dram_tensor("bo_a", [1, D], F32, kind="ExternalInput").ap()
    cos_d = nc.dram_tensor("cos_c", [256, S], F32, kind="ExternalInput").ap()
    sin_d = nc.dram_tensor("sin_c", [256, S], F32, kind="ExternalInput").ap()
    Wqkv_d = nc.dram_tensor("Wqkv_c", [D, 512], F32, kind="ExternalInput").ap()
    Wob_d = nc.dram_tensor("Wo_b", [D, D], F32, kind="ExternalInput").ap()
    wgT_d = nc.dram_tensor("wgT", [D, HID], I32, kind="ExternalInput").ap()
    wuT_d = nc.dram_tensor("wuT", [D, HID], I32, kind="ExternalInput").ap()
    wdT_d = nc.dram_tensor("wdT", [HID, D], I32, kind="ExternalInput").ap()
    tri_d = nc.dram_tensor("tri", [128, 128], BF16, kind="ExternalInput").ap()
    id_d = nc.dram_tensor("ident", [128, 128], BF16, kind="ExternalInput").ap()

    y_out = nc.dram_tensor("y_out", [T_OWN, D], F32, kind="ExternalOutput").ap()

    agin_a_h = nc.dram_tensor("agin_a", [512, S], BF16)
    agout_a_h = nc.dram_tensor("agout_a", [1024, S], BF16)
    agin_x_h = nc.dram_tensor("agin_x", [T_OWN, D], F32)
    agout_x_h = nc.dram_tensor("agout_x", [S, D], F32)
    agin_b_h = nc.dram_tensor("agin_b", [512, S], BF16)
    agout_b_h = nc.dram_tensor("agout_b", [1024, S], BF16)
    agin_a, agout_a = agin_a_h.ap(), agout_a_h.ap()
    agin_x, agout_x = agin_x_h.ap(), agout_x_h.ap()
    agin_b, agout_b = agin_b_h.ap(), agout_b_h.ap()
    groups = [[2 * i, 2 * i + 1] for i in range(n_pairs)]

    dbg_outs = {}

    def dbg_dram(name, shape, dt):
        if name not in dbg_outs:
            dbg_outs[name] = nc.dram_tensor(name, shape, dt, kind="ExternalOutput").ap()
        return dbg_outs[name]

    from contextlib import ExitStack
    with ExitStack() as ctx:
        E = ctx.enter_context
        mega = E(nc.sbuf_tensor("mega", [128, W16], BF16))
        idb = E(nc.sbuf_tensor("idb", [128, 128], BF16))
        trib = E(nc.sbuf_tensor("trib", [128, 128], BF16))
        ones1 = E(nc.sbuf_tensor("ones1", [1, 512], BF16))
        ones1f = E(nc.sbuf_tensor("ones1f", [1, 128], F32))
        bqcol = E(nc.sbuf_tensor("bqcol_s", [128, 4], F32))
        bkcol = E(nc.sbuf_tensor("bkcol_s", [128, 4], F32))
        biaso = E(nc.sbuf_tensor("biaso", [1, 1024], BF16))
        e0b = E(nc.sbuf_tensor("e0b", [128, 128], BF16))
        cA = E(nc.sbuf_tensor("cA", [128, 1], F32))
        cB = E(nc.sbuf_tensor("cB", [128, 1], F32))
        cC = E(nc.sbuf_tensor("cC", [128, 1], F32))
        cD = E(nc.sbuf_tensor("cD", [128, 1], F32))
        cE = E(nc.sbuf_tensor("cE", [128, 1], F32))
        stats = E(nc.sbuf_tensor("stats", [128, 8], F32))
        ps32 = E(nc.psum_tensor("ps32", [128, 3072], F32))
        psbf = E(nc.psum_tensor("psbf", [128, 2048], BF16))

        sems = {n: E(nc.semaphore("sem_" + n))
                for n in ("t", "v", "s", "cc", "gx",
                          "y0", "y1", "y2", "y3", "y4",
                          "g0", "g1", "g2", "g3")}

        Sd = Sched()
        P = Psum(Sd)
        REG = {}

        def m16(c0, cols):
            return mega[:, c0:c0 + cols]

        def m32(w0, words):
            return mega[:, R_WD + 2 * w0: R_WD + 2 * (w0 + words)].bitcast(F32)

        def bank32(b, n=512):
            return ps32[:, 512 * b: 512 * b + n]

        def recip_f():
            return m32(SC_XT, 1024)[0:1, :]

        def t_do(fn, inc=False):
            return Sd.do("t", fn, "t" if inc else None)

        def v_do(fn, inc=False):
            return Sd.do("v", fn, "v" if inc else None)

        def s_do(fn, inc=False):
            return Sd.do("s", fn, "s" if inc else None)

        chan_last = {}

        def dma_unit(eng, ch, transfers, gates=()):
            Sd.wait(eng, (ch, chan_last.get(ch, 0)))
            for gev in gates:
                Sd.wait(eng, gev)
            ev = None
            for fn in transfers:
                ev = Sd.do(eng, fn, ch, 16)
            chan_last[ch] = ev[1]
            return ev

        def g_dma(out, in_, ch="g0", gates=()):
            return dma_unit("g", ch, [lambda g: g.dma_start(out=out, in_=in_)], gates)

        def y_dma(out, in_, ch="y4", gates=()):
            return dma_unit("y", ch, [lambda y: y.dma_start(out=out, in_=in_)], gates)

        def g_other(fn):
            return Sd.do("g", fn, "gx", 1)

        def tfront():
            return ("t", Sd.counts.get("t", 0))

        def vfront():
            return ("v", Sd.counts.get("v", 0))

        def sfront():
            return ("s", Sd.counts.get("s", 0))

        def phase_gate():
            tf, vf, sf = tfront(), vfront(), sfront()
            Sd.wait("t", vf)
            Sd.wait("t", sf)
            Sd.wait("v", tf)
            Sd.wait("v", sf)
            Sd.wait("s", tf)
            Sd.wait("s", vf)

        def setup_regs(g):
            REG["own"] = g.alloc_register("own_off")
            REG["tmp"] = g.alloc_register("tmp_off")
            g.reg_mov(REG["own"], g.partition_id())
            g.reg_alu(REG["own"], REG["own"], 1, ALU.bitwise_and)
            g.reg_alu(REG["own"], REG["own"], 2048, ALU.mult)

        Sd.do("g", setup_regs)
        g_other(lambda g: g.memset(ones1[:, :], 1.0))
        g_other(lambda g: g.memset(ones1f[:, :], 1.0))
        g_other(lambda g: g.memset(cA[:, :], 0.125))
        g_other(lambda g: g.memset(cB[:, :], -SHIFT))
        g_other(lambda g: g.memset(cC[:, :], 1.0 / QS))
        g_other(lambda g: g.memset(cD[:, :], 1e-5))
        ev_init = g_other(lambda g: g.memset(cE[:, :], 1.0))
        g_dma(idb[:, :], id_d)
        g_dma(trib[:, :], tri_d)
        g_dma(e0b[:, :], e0_d)
        g_dma(bqcol[:, :], bqc_d)
        g_dma(bkcol[:, :], bkc_d)
        ev_rpz = g_other(lambda g: g.memset(mega[:, RECIP_PAD: RECIP_PAD + 1024], 0.0))
        ev_bvz = g_other(lambda g: g.memset(mega[:, BV_PAD: BV_PAD + 512], 0.0))
        Sd.wait("g", ev_bvz)
        g_dma(mega[0:1, BV_PAD: BV_PAD + 512], bv_d)
        ev_bias = g_dma(biaso[:, :], boa_d)
        Sd.wait("v", ev_rpz)

        for _e in ("t", "v", "s"):
            Sd.wait(_e, ev_init)
            Sd.wait(_e, ev_bias)

        v65_ones = mega[:, R_V65: R_V65 + 32 * 520].rearrange(
            "p (kt h x) -> p kt h x", h=8, x=65)[:, :, :, 64:65]
        ev_v65ones = [g_other(lambda g: g.memset(v65_ones, 1.0))]

        def load_w_kxm(dram, c0, ncols, nk=8, gates=()):
            out = mega[:, c0: c0 + nk * ncols].rearrange("p (k m) -> p k m", k=nk)
            return g_dma(out, dram.rearrange("(k p) m -> p k m", p=128), gates=gates)

        def ln_core(x_ap, tt, out_view, stage_base=XT_STG, junk_base=JUNK_A):
            st_sum = stats[:, 0:1]
            st_sq = stats[:, 1:2]
            st_mean = stats[:, 2:3]
            st_var = stats[:, 3:4]
            st_inv = stats[:, 4:5]
            junk = m16(junk_base, 1024)
            junk2 = m16(junk_base + 1024, 1024)
            v_do(lambda v: v.tensor_scalar(out=junk[:, :], in0=x_ap, scalar1=1.0,
                                           scalar2=0.0, op0=ALU.mult, op1=ALU.add,
                                           accum_out=st_sum))
            ev_acc = v_do(lambda v: v.scalar_tensor_tensor(out=junk2[:, :], in0=x_ap, scalar=1.0,
                                                           in1=x_ap, op0=ALU.mult, op1=ALU.mult,
                                                           accum_out=st_sq), inc=True)
            Sd.wait("v", ev_acc)
            ev_m = v_do(lambda v: v.tensor_scalar(out=st_mean, in0=st_sum, scalar1=1.0 / D,
                                                  scalar2=0.0, op0=ALU.mult, op1=ALU.add), inc=True)
            Sd.wait("v", ev_m)
            ev_m2 = v_do(lambda v: v.tensor_tensor(out=st_var, in0=st_mean, in1=st_mean,
                                                   op=ALU.mult), inc=True)
            Sd.wait("v", ev_m2)
            ev_v = v_do(lambda v: v.tensor_scalar(out=st_var, in0=st_sq, scalar1=1.0 / D,
                                                  scalar2=st_var, op0=ALU.mult,
                                                  op1=ALU.subtract), inc=True)
            Sd.wait("s", ev_v)
            st_sd = stats[:, 5:6]
            ev_sd = s_do(lambda s: s.activation(out=st_sd, in_=st_var, func=AF.Sqrt,
                                                scale=cE[:, :], bias=cD[:, :]), inc=True)
            Sd.wait("v", ev_sd)
            ev_rcp = v_do(lambda v: v.reciprocal(out=st_inv, in_=st_sd), inc=True)
            Sd.wait("v", ev_rcp)
            xln_tok = m16(stage_base + 1024 * (tt % 2), 1024)
            ev_app = v_do(lambda v: v.tensor_scalar(out=xln_tok, in0=x_ap, scalar1=st_mean,
                                                    scalar2=st_inv, op0=ALU.subtract,
                                                    op1=ALU.mult), inc=True)
            bank = 6 + (tt % 2)
            bcols = psbf[:, 1024 * (tt % 2): 1024 * (tt % 2) + 1024]
            Sd.wait("t", ev_app)
            P.guard_write(bank)
            ev_tr = None
            for c in range(8):
                ev_tr = t_do(lambda t, c=c, bcols=bcols, xln_tok=xln_tok: t.transpose(
                    out=bcols[:, 128 * c: 128 * c + 128],
                    in_=xln_tok[:, 128 * c: 128 * c + 128],
                    identity=idb[:, :]), inc=(c == 7))
            P.set_write(bank, ev_tr)
            P.guard_read("v", bank)
            ev_ev = v_do(lambda v, out_view=out_view, bcols=bcols: v.tensor_copy(
                out_view, bcols.rearrange("p (c x) -> p c x", c=8)), inc=True)
            P.set_read(bank, ev_ev)
            return ev_ev, ev_app

        xt_gate = {0: None, 1: None}

        def ln_tile_dram(src_rows, tt, out_view):
            par = tt % 2
            xt = m32(SC_XT + 1024 * par, 1024)
            g = [xt_gate[par]] if xt_gate[par] else []
            ld = y_dma(xt, src_rows, ch="y%d" % par, gates=g)
            Sd.wait("v", ld)
            ev_ev, ev_app = ln_core(xt, tt, out_view)
            xt_gate[par] = ev_app
            return ev_ev

        def phase_proj(is_b):
            phase_gate()
            blk_gate = {0: None, 1: None}
            tab_gate = {0: None, 1: None}
            if is_b:
                evw_q = load_w_kxm(Wqkv_d, R_W, 512)
                evw_k = evw_v = None
                wq_sb = [mega[:, R_W + 512 * k: R_W + 512 * k + 512] for k in range(8)]
                wk_sb = wv_sb = wq_sb
            else:
                evw_q = load_w_kxm(Wq_d, R_W, 512)
                evw_k = load_w_kxm(Wk_d, R_W + 4096, 512)
                evw_v = load_w_kxm(Wv_d, R_W + 8192, 512)
                wq_sb = [mega[:, R_W + 512 * k: R_W + 512 * k + 512] for k in range(8)]
                wk_sb = [mega[:, R_W + 4096 + 512 * k: R_W + 4096 + 512 * k + 512] for k in range(8)]
                wv_sb = [mega[:, R_W + 8192 + 512 * k: R_W + 8192 + 512 * k + 512] for k in range(8)]

            for blk in range(8):
                par = blk % 2
                xb_base = R_XLN + 4096 * par
                Sd.wait("v", blk_gate[par])
                for i in range(4):
                    tt = 4 * blk + i
                    ov = mega[:, xb_base: xb_base + 4096].rearrange(
                        "p (c x) -> p c x", x=512)[:, :, 128 * i:128 * i + 128]
                    src = (agout_x if is_b else xseq)[128 * tt:128 * tt + 128, :]
                    ev_ln = ln_tile_dram(src, tt, ov)
                xln_blk = [mega[:, xb_base + 512 * k: xb_base + 512 * k + 512] for k in range(8)]
                Sd.wait("t", ev_ln)

                if not is_b:
                    Sd.wait_all("t", [evw_q, evw_k, ev_bias, ev_init])
                    for j in range(2):
                        tpar = (2 * blk + j) % 2
                        cos_t = m32(SC_TAB + 1024 * tpar, 512)
                        sin_t = m32(SC_TAB + 1024 * tpar + 512, 512)
                        gts = [tab_gate[tpar]] if tab_gate[tpar] else []
                        ev_tab = dma_unit("g", "g%d" % (1 + tpar), [
                            lambda g, j=j, blk=blk, cos_t=cos_t: g.dma_start(
                                out=cos_t, in_=cos_d[128 * j:128 * j + 128, 512 * blk:512 * blk + 512]),
                            lambda g, j=j, blk=blk, sin_t=sin_t: g.dma_start(
                                out=sin_t, in_=sin_d[128 * j:128 * j + 128, 512 * blk:512 * blk + 512]),
                        ], gates=gts)
                        ev_c = ev_s2 = ev_tab
                        for (w_sb, bias_col, out_base) in ((wq_sb, bqcol, Q_FM), (wk_sb, bkcol, K_FM)):
                            for (bi, m) in ((0, j), (1, j + 2)):
                                P.guard_write(bi)
                                ev_mm = None
                                for k in range(8):
                                    ev_mm = t_do(lambda t, k=k, m=m, bi=bi, w_sb=w_sb, xln_blk=xln_blk: t.matmul(
                                        out=bank32(bi), lhsT=w_sb[k][:, 128 * m:128 * m + 128],
                                        rhs=xln_blk[k], start=(k == 0), stop=(k == 7)),
                                        inc=(k == 7))
                                P.set_write(bi, ev_mm)
                            Sd.wait_all("v", [ev_c, ev_s2])
                            P.guard_read("v", 0)
                            P.guard_read("v", 1)
                            t1 = m32(SC_T12, 512)
                            t2 = m32(SC_T12 + 512, 512)
                            t3 = m32(SC_T12 + 1024, 512)
                            t4 = m32(SC_T12 + 1536, 512)
                            o1 = mega[:, out_base + 4096 * j + 512 * blk:
                                      out_base + 4096 * j + 512 * blk + 512]
                            o2 = mega[:, out_base + 4096 * (j + 2) + 512 * blk:
                                      out_base + 4096 * (j + 2) + 512 * blk + 512]
                            be = bias_col[:, j:j + 1]
                            bo_ = bias_col[:, j + 2:j + 3]
                            v_do(lambda v, cos_t=cos_t, t1=t1, be=be: v.scalar_tensor_tensor(
                                out=t1, in0=bank32(0), scalar=be, in1=cos_t, op0=ALU.add, op1=ALU.mult))
                            v_do(lambda v, sin_t=sin_t, t2=t2, bo_=bo_: v.scalar_tensor_tensor(
                                out=t2, in0=bank32(1), scalar=bo_, in1=sin_t, op0=ALU.add, op1=ALU.mult))
                            v_do(lambda v, sin_t=sin_t, t3=t3, be=be: v.scalar_tensor_tensor(
                                out=t3, in0=bank32(0), scalar=be, in1=sin_t, op0=ALU.add, op1=ALU.mult))
                            ev_t4 = v_do(lambda v, cos_t=cos_t, t4=t4, bo_=bo_: v.scalar_tensor_tensor(
                                out=t4, in0=bank32(1), scalar=bo_, in1=cos_t, op0=ALU.add, op1=ALU.mult), inc=True)
                            Sd.wait("v", ev_t4)
                            v_do(lambda v, o1=o1, t1=t1, t2=t2: v.tensor_tensor(out=o1, in0=t1, in1=t2, op=ALU.subtract))
                            ev_r2 = v_do(lambda v, o2=o2, t3=t3, t4=t4: v.tensor_tensor(out=o2, in0=t3, in1=t4, op=ALU.subtract), inc=True)
                            P.set_read(0, ev_r2)
                            P.set_read(1, ev_r2)
                            tab_gate[tpar] = ev_r2
                else:
                    Sd.wait_all("t", [evw_q, ev_init])
                    for m in range(4):
                        bi = m % 2
                        P.guard_write(bi)
                        ev_mm = None
                        for k in range(8):
                            ev_mm = t_do(lambda t, k=k, m=m, bi=bi, xln_blk=xln_blk: t.matmul(
                                out=bank32(bi), lhsT=wq_sb[k][:, 128 * m:128 * m + 128],
                                rhs=xln_blk[k], start=(k == 0), stop=(k == 7)), inc=(k == 7))
                        P.set_write(bi, ev_mm)
                        P.guard_read("v", bi)
                        o1 = mega[:, Q_FM + 4096 * m + 512 * blk: Q_FM + 4096 * m + 512 * blk + 512]
                        ev_e = v_do(lambda v, bi=bi, o1=o1: v.tensor_copy(o1, bank32(bi)), inc=True)
                        P.set_read(bi, ev_e)

                if not is_b:
                    Sd.wait("t", evw_v)
                for i in range(4):
                    tt = 4 * blk + i
                    bi = 4 + (i % 2)
                    P.guard_write(bi)
                    for k in range(8):
                        ev_mm = t_do(lambda t, k=k, i=i, bi=bi, wv_sb=wv_sb, xln_blk=xln_blk: t.matmul(
                            out=bank32(bi), lhsT=xln_blk[k][:, 128 * i:128 * i + 128],
                            rhs=wv_sb[k], start=(k == 0), stop=(is_b and k == 7)),
                            inc=(is_b and k == 7))
                    if not is_b:
                        ev_mm = t_do(lambda t, bi=bi: t.matmul(
                            out=bank32(bi), lhsT=e0b[:, :],
                            rhs=mega[:, BV_PAD: BV_PAD + 512],
                            start=False, stop=True), inc=True)
                    P.set_write(bi, ev_mm)
                    P.guard_read("v", bi)
                    Sd.wait_all("v", ev_v65ones)
                    ov = mega[:, R_V65 + 520 * tt: R_V65 + 520 * tt + 520].rearrange(
                        "p (h x) -> p h x", x=65)[:, :, 0:64]
                    ev_e = v_do(lambda v, bi=bi, ov=ov: v.tensor_copy(
                        ov, bank32(bi).rearrange("p (h x) -> p h x", h=8)), inc=True)
                    P.set_read(bi, ev_e)
                blk_gate[par] = tfront()

        def phase_attn(is_b, agin):
            phase_gate()
            q_base = Q_FM
            k_base = Q_FM if is_b else K_FM
            ev_stage_dma = {}
            evs_dma = []
            unit = 0
            for c in range(4):
                for qo in range(8):
                    kt_max = 32 if is_b else 4 * (qo + 1)
                    for b in range(2):
                        P.guard_write(b)
                    first = [True, True]
                    pb_ev_ctx = [None, None, None, None]
                    pending = []
                    ev_ctx_last = None

                    def emit_ctx(pend):
                        nonlocal ev_ctx_last
                        kt, c0, pb, ev_exp, ev_mask = pend
                        Sd.wait("t", ev_exp)
                        Sd.wait("t", ev_mask)
                        for h in range(2):
                            st = first[h]
                            first[h] = False
                            ev_ctx_last = t_do(lambda t, kt=kt, c0=c0, pb=pb, h=h, st=st,
                                               sp=(kt == kt_max - 1), c=c: t.matmul(
                                out=ps32[0:65, 512 * h + c0: 512 * h + 512],
                                lhsT=mega[:, R_V65 + 520 * kt + 65 * (2 * c + h):
                                          R_V65 + 520 * kt + 65 * (2 * c + h) + 65],
                                rhs=pb[:, 512 * h + c0: 512 * h + 512],
                                start=st, stop=sp), inc=(h == 1))
                        pb_ev_ctx[kt % 4] = ev_ctx_last

                    for kt in range(kt_max):
                        c0 = 0 if is_b else max(0, kt * 128 - 512 * qo)
                        pb = m16(P_BUF + 1024 * (kt % 4), 1024)
                        sp2 = kt % 2
                        sreg0 = 1024 + 1024 * sp2
                        ev_sm = None
                        for h in range(2):
                            ol = 64 * h
                            P.guard_write(2 + 2 * sp2 + h)
                            ev_sm = t_do(lambda t, kt=kt, c0=c0, h=h, ol=ol, c=c, qo=qo,
                                         sreg0=sreg0: t.matmul(
                                out=ps32[:, sreg0 + 512 * h + c0: sreg0 + 512 * h + 512],
                                lhsT=mega[ol:ol + 64, k_base + c * 4096 + 128 * kt:
                                          k_base + c * 4096 + 128 * kt + 128],
                                rhs=mega[ol:ol + 64, q_base + c * 4096 + 512 * qo + c0:
                                         q_base + c * 4096 + 512 * qo + 512],
                                start=True, stop=True), inc=(h == 1))
                        for h in range(2):
                            P.set_write(2 + 2 * sp2 + h, ev_sm)
                        Sd.wait("s", pb_ev_ctx[kt % 4])
                        P.guard_read("s", 2 + 2 * sp2)
                        P.guard_read("s", 3 + 2 * sp2)
                        ev_exp = s_do(lambda s, sreg0=sreg0, pb=pb: s.activation(
                            out=pb[:, 0:1024], in_=ps32[:, sreg0: sreg0 + 1024],
                            func=AF.Exp, scale=cA[:, :], bias=cB[:, :]), inc=True)
                        P.set_read(2 + 2 * sp2, ev_exp)
                        P.set_read(3 + 2 * sp2, ev_exp)
                        ev_mask = None
                        if (not is_b) and kt * 128 >= 512 * qo:
                            Sd.wait("v", ev_exp)
                            v_do(lambda v, c0=c0, pb=pb: v.tensor_tensor(
                                out=pb[:, c0:c0 + 128], in0=pb[:, c0:c0 + 128],
                                in1=trib[:, :], op=ALU.mult))
                            ev_mask = v_do(lambda v, c0=c0, pb=pb: v.tensor_tensor(
                                out=pb[:, 512 + c0:512 + c0 + 128],
                                in0=pb[:, 512 + c0:512 + c0 + 128],
                                in1=trib[:, :], op=ALU.mult), inc=True)
                        pending.append((kt, c0, pb, ev_exp, ev_mask))
                        if len(pending) == 3:
                            emit_ctx(pending.pop(0))
                            emit_ctx(pending.pop(0))
                    for pend in pending:
                        emit_ctx(pend)
                    pending = []
                    for b in range(2):
                        P.set_write(b, ev_ctx_last)
                    Sd.wait("v", ev_ctx_last)
                    ev_rc = v_do(lambda v: v.reciprocal(out=recip_f(),
                                                        in_=ps32[64:65, 0:1024]), inc=True)
                    Sd.wait("v", ev_rc)
                    ev_rcb = v_do(lambda v: v.tensor_copy(
                        mega[0:1, RECIP_PAD: RECIP_PAD + 1024], recip_f()), inc=True)
                    ev_sts = []
                    for h in range(2):
                        bci = 4 + h
                        P.guard_write(bci)
                        Sd.wait("t", ev_rcb)
                        ev_bc = t_do(lambda t, h=h, bci=bci: t.matmul(
                            out=bank32(bci), lhsT=e0b[:, :],
                            rhs=mega[:, RECIP_PAD + 512 * h: RECIP_PAD + 512 * h + 512],
                            start=True, stop=True), inc=True)
                        P.set_write(bci, ev_bc)
                        stg = m16(CTX_STG + 512 * (2 * (unit % 2) + h), 512)
                        Sd.wait("v", ev_stage_dma.get((unit % 2, h)))
                        P.guard_read("v", bci)
                        bc_sb = m32(SC_BC, 512)
                        ev_cp = v_do(lambda v, bci=bci, bc_sb=bc_sb: v.tensor_copy(
                            bc_sb[0:64, :], bank32(bci)[0:64, :]), inc=True)
                        P.set_read(bci, ev_cp)
                        Sd.wait("v", ev_cp)
                        ev_st = v_do(lambda v, h=h, stg=stg, bc_sb=bc_sb: v.tensor_tensor(
                            out=stg[0:64, :], in0=ps32[0:64, 512 * h: 512 * h + 512],
                            in1=bc_sb[0:64, :], op=ALU.mult), inc=True)
                        ev_sts.append(ev_st)
                        ev_d = y_dma(agin[64 * (2 * c + h): 64 * (2 * c + h) + 64,
                                          512 * qo: 512 * qo + 512],
                                     stg[0:64, :], ch="y%d" % (2 + unit % 2),
                                     gates=[ev_st])
                        ev_stage_dma[(unit % 2, h)] = ev_d
                        evs_dma.append(ev_d)
                    for b in range(2):
                        P.set_read(b, ev_sts[-1])
                    unit += 1
            return evs_dma

        def xres_view():
            return mega[:, R_QK: R_QK + 2 * 16384].bitcast(F32)

        def phase_outproj(agout_h, w_dram, has_bias, ev_cc):
            phase_gate()
            if has_bias:
                Sd.wait("g", tfront())
                Sd.wait("g", sfront())
                Sd.wait("g", vfront())
                ev_z = g_other(lambda g: g.memset(mega[:, BO_PAD: BO_PAD + 1024], 0.0))
                Sd.wait("v", ev_z)
                ev_bp = v_do(lambda v: v.tensor_copy(mega[0:1, BO_PAD: BO_PAD + 1024],
                                                     biaso[0:1, :]), inc=True)
                Sd.wait("t", ev_bp)
            evw = load_w_kxm(w_dram, R_W, 1024, gates=[tfront()])
            wo_sb = [mega[:, R_W + 1024 * k: R_W + 1024 * k + 1024] for k in range(8)]
            ctxf = [mega[:, R_V65 + 2048 * k: R_V65 + 2048 * k + 2048] for k in range(8)]

            def load_ctx_k(g, k):
                if k == 0:
                    g.reg_alu(REG["tmp"], REG["own"], 0, ALU.add)
                else:
                    g.reg_alu(REG["tmp"], REG["tmp"], 128 * S, ALU.add)
                return g.dma_start(out=ctxf[k],
                                   in_=bass.AP(agout_h, REG["tmp"], [[S, 128], [1, 2048]]))

            ev_ctx = dma_unit("g", "g0",
                              [lambda g, k=k: load_ctx_k(g, k) for k in range(8)],
                              gates=[ev_cc, tfront()])
            if "opj" in dbg:
                Sd.wait("y", ev_ctx)
                barrier_into("y")
                dump("d_ctxf", dbg_dram("d_ctxf", [128, 16384], BF16), m16(R_V65, 16384))
                dump("d_agout", dbg_dram("d_agout", [1024, S], BF16), None,
                     dram_src=agout_h.ap())
            Sd.wait_all("t", [evw, ev_ctx])
            ev_add = None
            for tt in range(16):
                for n2 in range(2):
                    bi = (2 * tt + n2) % 4
                    P.guard_write(bi)
                    for k in range(8):
                        ev_mm = t_do(lambda t, k=k, tt=tt, n2=n2, bi=bi: t.matmul(
                            out=bank32(bi), lhsT=ctxf[k][:, 128 * tt:128 * tt + 128],
                            rhs=wo_sb[k][:, 512 * n2:512 * n2 + 512],
                            start=(k == 0), stop=(not has_bias and k == 7)),
                            inc=(not has_bias and k == 7))
                    if has_bias:
                        ev_mm = t_do(lambda t, n2=n2, bi=bi: t.matmul(
                            out=bank32(bi), lhsT=e0b[:, :],
                            rhs=mega[:, BO_PAD + 512 * n2: BO_PAD + 512 * n2 + 1024 - 512],
                            start=False, stop=True), inc=True)
                    P.set_write(bi, ev_mm)
                    P.guard_read("v", bi)
                    if "ydump" in dbg and tt == 0 and n2 == 0:
                        yscr = m16(P_BUF, 1024).bitcast(F32)
                        ev_yd = v_do(lambda v, bi=bi, yscr=yscr: v.tensor_copy(
                            yscr, bank32(bi)), inc=True)
                        Sd.wait("y", ev_yd)
                        dump("d_y0", dbg_dram("d_y0", [128, 512], F32), yscr)
                    xr = xres_view()
                    ev_add = v_do(lambda v, tt=tt, n2=n2, bi=bi, xr=xr: v.tensor_tensor(
                        out=xr[:, 1024 * tt + 512 * n2: 1024 * tt + 512 * n2 + 512],
                        in0=xr[:, 1024 * tt + 512 * n2: 1024 * tt + 512 * n2 + 512],
                        in1=bank32(bi), op=ALU.add), inc=True)
                    P.set_read(bi, ev_add)
            return ev_add


        def phase_ffn():
            phase_gate()
            xr = xres_view()
            ev_wd = g_dma(mega[:, R_WD: R_WD + 24576].rearrange("p (k m) -> p k m", k=24),
                          wdT_d.rearrange("(k p) m -> p k m", p=128),
                          gates=[tfront(), vfront()])
            wd_sb = [mega[:, R_WD + 1024 * hc: R_WD + 1024 * hc + 1024] for hc in range(24)]
            ev_r = None
            for half in range(2):
                evs = []
                for i in range(8):
                    tt = 8 * half + i
                    ov = mega[:, R_XLN: R_XLN + 8192].rearrange(
                        "p (c x) -> p c x", x=1024)[:, :, 128 * i:128 * i + 128]
                    evs.append(ln_core(xr[:, 1024 * tt: 1024 * tt + 1024], tt, ov,
                                       stage_base=R_W + 5120, junk_base=R_W + 7168)[0])
                xlnh = [mega[:, R_XLN + 1024 * k: R_XLN + 1024 * k + 1024] for k in range(8)]
                Sd.wait("t", evs[-1])
                hv_evs = {0: vfront(), 1: vfront()}
                gs_gate = {0: vfront(), 1: vfront()}
                ev_h = None
                for hc in range(24):
                    par = hc % 2
                    wbase = R_W + 2048 * par
                    wgv = mega[:, wbase: wbase + 1024].rearrange("p (k m) -> p k m", k=8)
                    wuv = mega[:, wbase + 1024: wbase + 2048].rearrange("p (k m) -> p k m", k=8)
                    gts = [hv_evs[par]] if hv_evs[par] else []
                    ev_wu2 = dma_unit("g", "g%d" % (1 + par), [
                        lambda g, hc=hc, wgv=wgv: g.dma_start(
                            out=wgv, in_=wgT_d[:, 128 * hc:128 * hc + 128].rearrange(
                                "(k p) m -> p k m", p=128)),
                        lambda g, hc=hc, wuv=wuv: g.dma_start(
                            out=wuv, in_=wuT_d[:, 128 * hc:128 * hc + 128].rearrange(
                                "(k p) m -> p k m", p=128)),
                    ], gates=gts)
                    ev_g1 = ev_g2 = ev_wu2
                    wg_k = [mega[:, wbase + 128 * k: wbase + 128 * k + 128] for k in range(8)]
                    wu_k = [mega[:, wbase + 1024 + 128 * k: wbase + 1024 + 128 * k + 128] for k in range(8)]
                    Sd.wait_all("t", [ev_g1, ev_g2])
                    for tb in range(2):
                        big = (2 * tb) % 4
                        biu = big + 1
                        P.guard_write(big)
                        ev_gm = None
                        for k in range(8):
                            ev_gm = t_do(lambda t, k=k, tb=tb, big=big, wg_k=wg_k, xlnh=xlnh: t.matmul(
                                out=bank32(big), lhsT=wg_k[k],
                                rhs=xlnh[k][:, 512 * tb:512 * tb + 512],
                                start=(k == 0), stop=(k == 7)), inc=(k == 7))
                        P.set_write(big, ev_gm)
                        P.guard_write(biu)
                        ev_um = None
                        for k in range(8):
                            ev_um = t_do(lambda t, k=k, tb=tb, biu=biu, wu_k=wu_k, xlnh=xlnh: t.matmul(
                                out=bank32(biu), lhsT=wu_k[k],
                                rhs=xlnh[k][:, 512 * tb:512 * tb + 512],
                                start=(k == 0), stop=(k == 7)), inc=(k == 7))
                        P.set_write(biu, ev_um)
                        P.guard_read("s", big)
                        gpar = (2 * hc + tb) % 2
                        gs = m16(R_W + 4096 + 512 * gpar, 512)
                        Sd.wait("s", gs_gate[gpar])
                        ev_si = s_do(lambda s, big=big, gs=gs: s.activation(
                            out=gs, in_=bank32(big), func=AF.Silu, scale=cC[:, :]), inc=True)
                        P.set_read(big, ev_si)
                        hv = mega[:, R_V65 + 1024 * hc + 512 * tb:
                                  R_V65 + 1024 * hc + 512 * tb + 512]
                        Sd.wait("v", ev_si)
                        P.guard_read("v", biu)
                        ev_h = v_do(lambda v, biu=biu, gs=gs, hv=hv: v.tensor_tensor(
                            out=hv, in0=gs, in1=bank32(biu), op=ALU.mult), inc=True)
                        P.set_read(biu, ev_h)
                        gs_gate[gpar] = ev_h
                    hv_evs[par] = ev_h
                Sd.wait("t", ev_wd)
                Sd.wait("t", ev_h)
                for i in range(8):
                    tt = 8 * half + i
                    for n2 in range(2):
                        bi = (2 * i + n2) % 4
                        P.guard_write(bi)
                        ev_dm = None
                        for hc in range(24):
                            ev_dm = t_do(lambda t, hc=hc, i=i, n2=n2, bi=bi: t.matmul(
                                out=bank32(bi),
                                lhsT=mega[:, R_V65 + 1024 * hc + 128 * i:
                                          R_V65 + 1024 * hc + 128 * i + 128],
                                rhs=wd_sb[hc][:, 512 * n2:512 * n2 + 512],
                                start=(hc == 0), stop=(hc == 23)), inc=(hc == 23))
                        P.set_write(bi, ev_dm)
                        P.guard_read("v", bi)
                        ev_r = v_do(lambda v, tt=tt, n2=n2, bi=bi, xr=xr: v.scalar_tensor_tensor(
                            out=xr[:, 1024 * tt + 512 * n2: 1024 * tt + 512 * n2 + 512],
                            in0=bank32(bi), scalar=1.0 / (QS * QS),
                            in1=xr[:, 1024 * tt + 512 * n2: 1024 * tt + 512 * n2 + 512],
                            op0=ALU.mult, op1=ALU.add), inc=True)
                        P.set_read(bi, ev_r)
            return ev_r

        def dump(name, dram_ap, sb_view, dram_src=None):
            if dram_src is not None:
                ev = g_dma(dram_ap, dram_src)
                Sd.wait("g", ev)
            else:
                ev = y_dma(dram_ap, sb_view)
                Sd.wait("y", ev)

        def barrier_into(eng):
            for sem, c in Sd.counts.items():
                Sd.wait(eng, (sem, c))

        def main_tail():
            xr = xres_view()
            ev_w_x = g_dma(agin_x.rearrange("(a p) c -> p a c", p=128),
                           xr.rearrange("p (a c) -> p a c", a=16),
                           gates=[vfront()])
            Sd.wait("g", ev_w_x)
            ev_cc_x = Sd.do("g", lambda g: g.collective_compute(
                "AllGather", ALU.bypass, replica_groups=groups,
                ins=[agin_x.opt()], outs=[agout_x.opt()]), "cc", 1)
            Sd.wait("y", ev_cc_x)
            phase_proj(is_b=True)
            if "stop_qkvb" in dbg:
                barrier_into("y")
                dump("d_qkvb", dbg_dram("d_qkvb", [512, S], BF16).rearrange(
                    "(a p) c -> p a c", p=128),
                    m16(Q_FM, 16384).rearrange("p (a c) -> p a c", a=4))
                return
            evs_b = phase_attn(is_b=True, agin=agin_b)
            Sd.wait_all("g", evs_b)
            ev_cc_b = Sd.do("g", lambda g: g.collective_compute(
                "AllGather", ALU.bypass, replica_groups=groups,
                ins=[agin_b.opt()], outs=[agout_b.opt()]), "cc", 1)
            ev_xr2 = y_dma(xr.rearrange("p (a c) -> p a c", a=16),
                           agin_x.rearrange("(a p) c -> p a c", p=128),
                           gates=[tfront()])
            Sd.wait("v", ev_xr2)
            phase_outproj(agout_b_h, Wob_d, has_bias=False, ev_cc=ev_cc_b)
            if "x3" in dbg:
                barrier_into("y")
                dump("d_x3", dbg_dram("d_x3", [T_OWN, D], F32).rearrange("(a p) c -> p a c", p=128),
                     xr.rearrange("p (a c) -> p a c", a=16))
            if "stop_x3" in dbg:
                return
            phase_ffn()
            ev_out = y_dma(y_out.rearrange("(a p) c -> p a c", p=128),
                           xr.rearrange("p (a c) -> p a c", a=16),
                           gates=[vfront()])
            Sd.wait("y", ev_out)

        phase_proj(is_b=False)
        if "qkv" in dbg:
            barrier_into("y")
            dump("d_q", dbg_dram("d_q", [512, S], BF16).rearrange("(a p) c -> p a c", p=128),
                 m16(Q_FM, 16384).rearrange("p (a c) -> p a c", a=4))
            dump("d_k", dbg_dram("d_k", [512, S], BF16).rearrange("(a p) c -> p a c", p=128),
                 m16(K_FM, 16384).rearrange("p (a c) -> p a c", a=4))
            dump("d_v", dbg_dram("d_v", [128, 32 * 520], BF16),
                 m16(R_V65, 16640))
        if "stop_qkv" in dbg:
            barrier_into("y")
        elif "stop_ctxa" in dbg:
            evs_a = phase_attn(is_b=False, agin=agin_a)
            Sd.wait_all("g", evs_a)
            barrier_into("y")
            dump("d_ctxa", dbg_dram("d_ctxa", [512, S], BF16), None, dram_src=agin_a)
        else:
            evs_a = phase_attn(is_b=False, agin=agin_a)
            Sd.wait_all("g", evs_a)
            ev_cc_a = Sd.do("g", lambda g: g.collective_compute(
                "AllGather", ALU.bypass, replica_groups=groups,
                ins=[agin_a.opt()], outs=[agout_a.opt()]), "cc", 1)
            xr = xres_view()
            ev_xr = y_dma(xr.rearrange("p (a c) -> p a c", a=16),
                          xown.rearrange("(a p) c -> p a c", p=128),
                          gates=[tfront()])
            Sd.wait("v", ev_xr)
            phase_outproj(agout_a_h, Woa_d, has_bias=True, ev_cc=ev_cc_a)
            if "x1" in dbg:
                barrier_into("y")
                dump("d_x1", dbg_dram("d_x1", [T_OWN, D], F32).rearrange("(a p) c -> p a c", p=128),
                     xr.rearrange("p (a c) -> p a c", a=16))
            if "stop_x1" in dbg:
                return_early = True
            else:
                return_early = False
            if not return_early:
                phase_ffn()
                if "x2" in dbg:
                    barrier_into("y")
                    dump("d_x2", dbg_dram("d_x2", [T_OWN, D], F32).rearrange("(a p) c -> p a c", p=128),
                         xr.rearrange("p (a c) -> p a c", a=16))
            if return_early or "stop_x2" in dbg:
                pass
            else:
                main_tail()
        if False:
            pass

        with nc.Block() as block:
            @block.tensor
            def _(t):
                Sd.replay("t", t, sems)

            @block.vector
            def _(v):
                Sd.replay("v", v, sems)

            @block.scalar
            def _(s):
                Sd.replay("s", s, sems)

            @block.gpsimd
            def _(g):
                Sd.replay("g", g, sems)

            @block.sync
            def _(y):
                Sd.replay("y", y, sems)

        ninstr = {e: len(Sd.cmds[e]) for e in Sd.ENGS}
    return nc, ninstr


def host_prep(inputs):
    Wq, bq = np.asarray(inputs["L0_attn_a_Wq"]), np.asarray(inputs["L0_attn_a_bq"])
    Wk, bk = np.asarray(inputs["L0_attn_a_Wk"]), np.asarray(inputs["L0_attn_a_bk"])
    Wv, bv = np.asarray(inputs["L0_attn_a_Wv"]), np.asarray(inputs["L0_attn_a_bv"])
    Wo, bo = np.asarray(inputs["L0_attn_a_Wo"]), np.asarray(inputs["L0_attn_a_bo"])
    Wqkv = np.asarray(inputs["L0_attn_b_W_qkv"])
    Wob = np.asarray(inputs["L0_attn_b_W_o"])
    wg = np.asarray(inputs["L0_ffn_w_gate"])
    wu = np.asarray(inputs["L0_ffn_w_up"])
    wd = np.asarray(inputs["L0_ffn_w_down"])
    x0 = np.asarray(inputs["L0_attn_a_x"])

    j = np.arange(512, dtype=np.float32)
    freq_cis = np.power(np.float32(10000.0), np.float32(-2.0) * j / np.float32(D))
    pos = np.arange(S, dtype=np.float32)
    freq = np.outer(pos, freq_cis).astype(np.float32)
    cosf = np.cos(freq).astype(np.float32)
    sinf = np.sin(freq).astype(np.float32)

    perm = np.empty(1024, dtype=np.int64)
    for pp in (0, 1):
        for d_ in range(512):
            r = 512 * pp + d_
            perm[r] = (256 * pp + d_) if d_ < 256 else (512 + 256 * pp + (d_ - 256))
    Wo_perm = np.ascontiguousarray(Wo[perm, :])

    tri = np.triu(np.ones((128, 128), np.float32)).astype(ml_dtypes.bfloat16)
    ident = np.eye(128, dtype=ml_dtypes.bfloat16)
    e0 = np.zeros((128, 128), dtype=ml_dtypes.bfloat16)
    e0[0, :] = 1
    wgT = np.ascontiguousarray(wg.T)
    wuT = np.ascontiguousarray(wu.T)
    wdT = np.ascontiguousarray(wd.T)
    WobT = np.ascontiguousarray(Wob.T)

    per_core = []
    for core in range(8):
        s, p = core // 2, core % 2
        sl = slice(512 * p, 512 * p + 512)
        Wq_c = np.ascontiguousarray(np.concatenate(
            [Wq[:, sl][:, 0::2], Wq[:, sl][:, 1::2]], axis=1))
        Wk_c = np.ascontiguousarray(np.concatenate(
            [Wk[:, sl][:, 0::2], Wk[:, sl][:, 1::2]], axis=1))
        bq_cc = np.concatenate([bq[sl][0::2], bq[sl][1::2]]).astype(np.float32)
        bk_cc = np.concatenate([bk[sl][0::2], bk[sl][1::2]]).astype(np.float32)
        bqcol = np.ascontiguousarray(bq_cc.reshape(4, 128).T)
        bkcol = np.ascontiguousarray(bk_cc.reshape(4, 128).T)
        gheads = [4 * p + l for l in range(4)] + [8 + 4 * p + l for l in range(4)]
        vcols = np.concatenate([np.arange(64 * g, 64 * g + 64) for g in gheads])
        Wv_c = np.ascontiguousarray(Wv[:, vcols])
        bv_c = np.ascontiguousarray(bv[vcols])[None, :]
        jsl = slice(256 * p, 256 * p + 256)
        cos_c = np.ascontiguousarray(cosf[:, jsl].T)
        sin_c = np.ascontiguousarray(sinf[:, jsl].T)
        Wqkv_c = np.ascontiguousarray(Wqkv[512 * p: 512 * p + 512, :].T)
        per_core.append({
            "xseq": np.ascontiguousarray(x0[s]),
            "xown": np.ascontiguousarray(x0[s][2048 * p: 2048 * p + 2048]),
            "Wq_c": Wq_c, "Wk_c": Wk_c, "Wv_c": Wv_c,
            "bqcol": bqcol, "bkcol": bkcol,
            "bv_c": np.ascontiguousarray(bv_c),
            "e0": e0,
            "Wo_a": Wo_perm, "bo_a": np.ascontiguousarray(bo[None, :]),
            "cos_c": cos_c, "sin_c": sin_c,
            "Wqkv_c": Wqkv_c, "Wo_b": WobT,
            "wgT": wgT, "wuT": wuT, "wdT": wdT,
            "tri": tri, "ident": ident,
        })
    return per_core


def kernel(**inputs):
    from concourse.bass_utils import run_bass_kernel_spmd
    per_core = host_prep(inputs)
    nc, _ = build()
    res = run_bass_kernel_spmd(nc, per_core, core_ids=list(range(8)))
    out = np.zeros((B, S, D), np.float32)
    for core in range(8):
        s, p = core // 2, core % 2
        out[s, 2048 * p: 2048 * p + 2048] = res.results[core]["y_out"]
    return out


# version 08
# speedup vs baseline: 1.0714x; 1.0714x over previous
import numpy as np
import ml_dtypes
import concourse.bass as bass
import concourse.mybir as mybir

F32 = mybir.dt.float32
BF16 = mybir.dt.bfloat16
I32 = mybir.dt.int32
AF = mybir.ActivationFunctionType
ALU = mybir.AluOpType

B, S, D, NH, DH, HID = 4, 4096, 1024, 16, 64, 3072
HL = 8
T_OWN = 2048
SHIFT = 12.0
QS = 127.0

R_XLN = 0
R_QK = 8192
R_V65 = 40960
R_CS = 57600
R_W = 65792
R_WD = 78080
W16 = 102656

Q_FM = R_QK
K_FM = R_QK + 16384
P_BUF = R_CS
CTX_STG = R_CS + 4096
RECIP_PAD = R_CS + 6144
BV_PAD = R_CS + 7168
BO_PAD = R_CS
XT_STG = R_CS
JUNK_A = R_CS + 4096

SC_XT = 0
SC_T12 = 2048
SC_TAB = 8192
SC_BC = 11264


class Sched:
    ENGS = ("t", "v", "s", "g", "y")

    def __init__(self):
        self.cmds = {e: [] for e in self.ENGS}
        self.counts = {}
        self.lastwait = {}

    def do(self, eng, fn, sem=None, amt=1):
        if sem is None:
            self.cmds[eng].append(("plain", fn))
            return None
        c = self.counts.get(sem, 0) + amt
        self.counts[sem] = c
        self.cmds[eng].append(("inc", fn, sem, amt))
        return (sem, c)

    def wait(self, eng, ev):
        if ev is None:
            return
        sem, val = ev
        key = (eng, sem)
        if self.lastwait.get(key, -1) >= val:
            return
        self.lastwait[key] = val
        self.cmds[eng].append(("wait", sem, val))

    def wait_all(self, eng, evs):
        for ev in evs:
            self.wait(eng, ev)

    def frontier(self):
        return [(s, c) for s, c in self.counts.items()]

    def replay(self, eng, handle, sems):
        for cmd in self.cmds[eng]:
            if cmd[0] == "plain":
                cmd[1](handle)
            elif cmd[0] == "inc":
                cmd[1](handle).then_inc(sems[cmd[2]], cmd[3])
            else:
                handle.wait_ge(sems[cmd[1]], cmd[2])


class Psum:
    def __init__(self, sched):
        self.S = sched
        self.write_ev = [None] * 8
        self.read_ev = [None] * 8

    def guard_write(self, bank):
        self.S.wait("t", self.read_ev[bank])

    def set_write(self, bank, ev):
        self.write_ev[bank] = ev

    def guard_read(self, eng, bank):
        self.S.wait(eng, self.write_ev[bank])

    def set_read(self, bank, ev):
        self.read_ev[bank] = ev


def build(debug=(), n_pairs=4):
    nc = bass.Bass("TRN2", target_bir_lowering=False)
    dbg = set(debug)

    xseq = nc.dram_tensor("xseq", [S, D], F32, kind="ExternalInput").ap()
    xown = nc.dram_tensor("xown", [T_OWN, D], F32, kind="ExternalInput").ap()
    Wq_d = nc.dram_tensor("Wq_c", [D, 512], F32, kind="ExternalInput").ap()
    Wk_d = nc.dram_tensor("Wk_c", [D, 512], F32, kind="ExternalInput").ap()
    Wv_d = nc.dram_tensor("Wv_c", [D, 512], F32, kind="ExternalInput").ap()
    bqc_d = nc.dram_tensor("bqcol", [128, 4], F32, kind="ExternalInput").ap()
    bkc_d = nc.dram_tensor("bkcol", [128, 4], F32, kind="ExternalInput").ap()
    bv_d = nc.dram_tensor("bv_c", [1, 512], F32, kind="ExternalInput").ap()
    e0_d = nc.dram_tensor("e0", [128, 128], BF16, kind="ExternalInput").ap()
    Woa_d = nc.dram_tensor("Wo_a", [D, D], F32, kind="ExternalInput").ap()
    boa_d = nc.dram_tensor("bo_a", [1, D], F32, kind="ExternalInput").ap()
    cos_d = nc.dram_tensor("cos_c", [256, S], F32, kind="ExternalInput").ap()
    sin_d = nc.dram_tensor("sin_c", [256, S], F32, kind="ExternalInput").ap()
    Wqkv_d = nc.dram_tensor("Wqkv_c", [D, 512], F32, kind="ExternalInput").ap()
    Wob_d = nc.dram_tensor("Wo_b", [D, D], F32, kind="ExternalInput").ap()
    wgT_d = nc.dram_tensor("wgT", [D, HID], I32, kind="ExternalInput").ap()
    wuT_d = nc.dram_tensor("wuT", [D, HID], I32, kind="ExternalInput").ap()
    wdT_d = nc.dram_tensor("wdT", [HID, D], I32, kind="ExternalInput").ap()
    tri_d = nc.dram_tensor("tri", [128, 128], BF16, kind="ExternalInput").ap()
    id_d = nc.dram_tensor("ident", [128, 128], BF16, kind="ExternalInput").ap()

    y_out = nc.dram_tensor("y_out", [T_OWN, D], F32, kind="ExternalOutput").ap()

    agin_a_h = nc.dram_tensor("agin_a", [512, S], BF16)
    agout_a_h = nc.dram_tensor("agout_a", [1024, S], BF16)
    agin_x_h = nc.dram_tensor("agin_x", [T_OWN, D], F32)
    agout_x_h = nc.dram_tensor("agout_x", [S, D], F32)
    agin_b_h = nc.dram_tensor("agin_b", [512, S], BF16)
    agout_b_h = nc.dram_tensor("agout_b", [1024, S], BF16)
    agin_a, agout_a = agin_a_h.ap(), agout_a_h.ap()
    agin_x, agout_x = agin_x_h.ap(), agout_x_h.ap()
    agin_b, agout_b = agin_b_h.ap(), agout_b_h.ap()
    groups = [[2 * i, 2 * i + 1] for i in range(n_pairs)]

    dbg_outs = {}

    def dbg_dram(name, shape, dt):
        if name not in dbg_outs:
            dbg_outs[name] = nc.dram_tensor(name, shape, dt, kind="ExternalOutput").ap()
        return dbg_outs[name]

    from contextlib import ExitStack
    with ExitStack() as ctx:
        E = ctx.enter_context
        mega = E(nc.sbuf_tensor("mega", [128, W16], BF16))
        idb = E(nc.sbuf_tensor("idb", [128, 128], BF16))
        trib = E(nc.sbuf_tensor("trib", [128, 128], BF16))
        ones1 = E(nc.sbuf_tensor("ones1", [1, 512], BF16))
        ones1f = E(nc.sbuf_tensor("ones1f", [1, 128], F32))
        bqcol = E(nc.sbuf_tensor("bqcol_s", [128, 4], F32))
        bkcol = E(nc.sbuf_tensor("bkcol_s", [128, 4], F32))
        biaso = E(nc.sbuf_tensor("biaso", [1, 1024], BF16))
        e0b = E(nc.sbuf_tensor("e0b", [128, 128], BF16))
        cA = E(nc.sbuf_tensor("cA", [128, 1], F32))
        cB = E(nc.sbuf_tensor("cB", [128, 1], F32))
        cC = E(nc.sbuf_tensor("cC", [128, 1], F32))
        cD = E(nc.sbuf_tensor("cD", [128, 1], F32))
        cE = E(nc.sbuf_tensor("cE", [128, 1], F32))
        stats = E(nc.sbuf_tensor("stats", [128, 16], F32))
        ps32 = E(nc.psum_tensor("ps32", [128, 4096], F32))

        sems = {n: E(nc.semaphore("sem_" + n))
                for n in ("t", "v", "s", "cc", "gx",
                          "y0", "y1", "y2", "y3", "y4",
                          "g0", "g1", "g2", "g3")}

        Sd = Sched()
        P = Psum(Sd)
        REG = {}

        def m16(c0, cols):
            return mega[:, c0:c0 + cols]

        def m32(w0, words):
            return mega[:, R_WD + 2 * w0: R_WD + 2 * (w0 + words)].bitcast(F32)

        def bank32(b, n=512):
            return ps32[:, 512 * b: 512 * b + n]

        def recip_f():
            return m32(SC_XT, 1024)[0:1, :]

        def t_do(fn, inc=False):
            return Sd.do("t", fn, "t" if inc else None)

        def v_do(fn, inc=False):
            return Sd.do("v", fn, "v" if inc else None)

        def s_do(fn, inc=False):
            return Sd.do("s", fn, "s" if inc else None)

        chan_last = {}

        def dma_unit(eng, ch, transfers, gates=()):
            Sd.wait(eng, (ch, chan_last.get(ch, 0)))
            for gev in gates:
                Sd.wait(eng, gev)
            ev = None
            for fn in transfers:
                ev = Sd.do(eng, fn, ch, 16)
            chan_last[ch] = ev[1]
            return ev

        def g_dma(out, in_, ch="g0", gates=()):
            return dma_unit("g", ch, [lambda g: g.dma_start(out=out, in_=in_)], gates)

        def y_dma(out, in_, ch="y4", gates=()):
            return dma_unit("y", ch, [lambda y: y.dma_start(out=out, in_=in_)], gates)

        def g_other(fn):
            return Sd.do("g", fn, "gx", 1)

        def tfront():
            return ("t", Sd.counts.get("t", 0))

        def vfront():
            return ("v", Sd.counts.get("v", 0))

        def sfront():
            return ("s", Sd.counts.get("s", 0))

        def phase_gate():
            tf, vf, sf = tfront(), vfront(), sfront()
            Sd.wait("t", vf)
            Sd.wait("t", sf)
            Sd.wait("v", tf)
            Sd.wait("v", sf)
            Sd.wait("s", tf)
            Sd.wait("s", vf)

        def setup_regs(g):
            REG["own"] = g.alloc_register("own_off")
            REG["tmp"] = g.alloc_register("tmp_off")
            g.reg_mov(REG["own"], g.partition_id())
            g.reg_alu(REG["own"], REG["own"], 1, ALU.bitwise_and)
            g.reg_alu(REG["own"], REG["own"], 2048, ALU.mult)

        Sd.do("g", setup_regs)
        g_other(lambda g: g.memset(ones1[:, :], 1.0))
        g_other(lambda g: g.memset(ones1f[:, :], 1.0))
        g_other(lambda g: g.memset(cA[:, :], 0.125))
        g_other(lambda g: g.memset(cB[:, :], -SHIFT))
        g_other(lambda g: g.memset(cC[:, :], 1.0 / QS))
        g_other(lambda g: g.memset(cD[:, :], 1e-5))
        ev_init = g_other(lambda g: g.memset(cE[:, :], 1.0))
        g_dma(idb[:, :], id_d)
        g_dma(trib[:, :], tri_d)
        g_dma(e0b[:, :], e0_d)
        g_dma(bqcol[:, :], bqc_d)
        g_dma(bkcol[:, :], bkc_d)
        ev_rpz = g_other(lambda g: g.memset(mega[:, RECIP_PAD: RECIP_PAD + 1024], 0.0))
        ev_bvz = g_other(lambda g: g.memset(mega[:, BV_PAD: BV_PAD + 512], 0.0))
        Sd.wait("g", ev_bvz)
        g_dma(mega[0:1, BV_PAD: BV_PAD + 512], bv_d)
        ev_bias = g_dma(biaso[:, :], boa_d)
        Sd.wait("v", ev_rpz)

        for _e in ("t", "v", "s"):
            Sd.wait(_e, ev_init)
            Sd.wait(_e, ev_bias)

        v65_ones = mega[:, R_V65: R_V65 + 32 * 520].rearrange(
            "p (kt h x) -> p kt h x", h=8, x=65)[:, :, :, 64:65]
        ev_v65ones = [g_other(lambda g: g.memset(v65_ones, 1.0))]

        def load_w_kxm(dram, c0, ncols, nk=8, gates=()):
            out = mega[:, c0: c0 + nk * ncols].rearrange("p (k m) -> p k m", k=nk)
            return g_dma(out, dram.rearrange("(k p) m -> p k m", p=128), gates=gates)

        def ln_core(x_ap, tt, out_view, stage_base=XT_STG, junk_base=JUNK_A):
            sb8 = 8 * (tt % 2)
            st_sum = stats[:, sb8 + 0:sb8 + 1]
            st_sq = stats[:, sb8 + 1:sb8 + 2]
            st_mean = stats[:, sb8 + 2:sb8 + 3]
            st_var = stats[:, sb8 + 3:sb8 + 4]
            st_inv = stats[:, sb8 + 4:sb8 + 5]
            junk = m16(junk_base, 1024)
            junk2 = m16(junk_base + 1024, 1024)
            v_do(lambda v: v.tensor_scalar(out=junk[:, :], in0=x_ap, scalar1=1.0,
                                           scalar2=0.0, op0=ALU.mult, op1=ALU.add,
                                           accum_out=st_sum))
            ev_acc = v_do(lambda v: v.scalar_tensor_tensor(out=junk2[:, :], in0=x_ap, scalar=1.0,
                                                           in1=x_ap, op0=ALU.mult, op1=ALU.mult,
                                                           accum_out=st_sq), inc=True)
            Sd.wait("v", ev_acc)
            ev_m = v_do(lambda v: v.tensor_scalar(out=st_mean, in0=st_sum, scalar1=1.0 / D,
                                                  scalar2=0.0, op0=ALU.mult, op1=ALU.add), inc=True)
            Sd.wait("v", ev_m)
            ev_m2 = v_do(lambda v: v.tensor_tensor(out=st_var, in0=st_mean, in1=st_mean,
                                                   op=ALU.mult), inc=True)
            Sd.wait("v", ev_m2)
            ev_v = v_do(lambda v: v.tensor_scalar(out=st_var, in0=st_sq, scalar1=1.0 / D,
                                                  scalar2=st_var, op0=ALU.mult,
                                                  op1=ALU.subtract), inc=True)
            Sd.wait("s", ev_v)
            st_sd = stats[:, sb8 + 5:sb8 + 6]
            ev_sd = s_do(lambda s: s.activation(out=st_sd, in_=st_var, func=AF.Sqrt,
                                                scale=cE[:, :], bias=cD[:, :]), inc=True)
            Sd.wait("v", ev_sd)
            ev_rcp = v_do(lambda v: v.reciprocal(out=st_inv, in_=st_sd), inc=True)
            Sd.wait("v", ev_rcp)
            xln_tok = m16(stage_base + 1024 * (tt % 2), 1024)
            ev_app = v_do(lambda v: v.tensor_scalar(out=xln_tok, in0=x_ap, scalar1=st_mean,
                                                    scalar2=st_inv, op0=ALU.subtract,
                                                    op1=ALU.mult), inc=True)
            bank = 6 + (tt % 2)
            bcols = ps32[:, 3072 + 512 * (tt % 2): 3072 + 512 * (tt % 2) + 512].bitcast(BF16)
            Sd.wait("t", ev_app)
            P.guard_write(bank)
            ev_tr = None
            for c in range(8):
                ev_tr = t_do(lambda t, c=c, bcols=bcols, xln_tok=xln_tok: t.transpose(
                    out=bcols[:, 128 * c: 128 * c + 128],
                    in_=xln_tok[:, 128 * c: 128 * c + 128],
                    identity=idb[:, :]), inc=(c == 7))
            P.set_write(bank, ev_tr)
            P.guard_read("v", bank)
            ev_ev = v_do(lambda v, out_view=out_view, bcols=bcols: v.tensor_copy(
                out_view, bcols.rearrange("p (c x) -> p c x", c=8)), inc=True)
            P.set_read(bank, ev_ev)
            return ev_ev, ev_app

        xt_gate = {0: None, 1: None}

        def ln_tile_dram(src_rows, tt, out_view):
            par = tt % 2
            xt = m32(SC_XT + 1024 * par, 1024)
            g = [xt_gate[par]] if xt_gate[par] else []
            ld = y_dma(xt, src_rows, ch="y%d" % par, gates=g)
            Sd.wait("v", ld)
            ev_ev, ev_app = ln_core(xt, tt, out_view)
            xt_gate[par] = ev_app
            return ev_ev

        def phase_proj(is_b):
            phase_gate()
            blk_gate = {0: None, 1: None}
            tab_gate = {0: None, 1: None}
            if is_b:
                evw_q = load_w_kxm(Wqkv_d, R_W, 512)
                evw_k = evw_v = None
                wq_sb = [mega[:, R_W + 512 * k: R_W + 512 * k + 512] for k in range(8)]
                wk_sb = wv_sb = wq_sb
            else:
                evw_q = load_w_kxm(Wq_d, R_W, 512)
                evw_k = load_w_kxm(Wk_d, R_W + 4096, 512)
                evw_v = load_w_kxm(Wv_d, R_W + 8192, 512)
                wq_sb = [mega[:, R_W + 512 * k: R_W + 512 * k + 512] for k in range(8)]
                wk_sb = [mega[:, R_W + 4096 + 512 * k: R_W + 4096 + 512 * k + 512] for k in range(8)]
                wv_sb = [mega[:, R_W + 8192 + 512 * k: R_W + 8192 + 512 * k + 512] for k in range(8)]

            for blk in range(8):
                par = blk % 2
                xb_base = R_XLN + 4096 * par
                Sd.wait("v", blk_gate[par])
                for i in range(4):
                    tt = 4 * blk + i
                    ov = mega[:, xb_base: xb_base + 4096].rearrange(
                        "p (c x) -> p c x", x=512)[:, :, 128 * i:128 * i + 128]
                    if is_b:
                        g0 = 128 * tt
                        row = 2048 * ((g0 % 2048) // 1024) + 1024 * (g0 // 2048) + (g0 % 1024)
                        src = agout_x[row:row + 128, :]
                    else:
                        src = xseq[128 * tt:128 * tt + 128, :]
                    ev_ln = ln_tile_dram(src, tt, ov)
                xln_blk = [mega[:, xb_base + 512 * k: xb_base + 512 * k + 512] for k in range(8)]
                Sd.wait("t", ev_ln)

                if not is_b:
                    Sd.wait_all("t", [evw_q, evw_k, ev_bias, ev_init])
                    for j in range(2):
                        tpar = (2 * blk + j) % 2
                        cos_t = m32(SC_TAB + 1024 * tpar, 512)
                        sin_t = m32(SC_TAB + 1024 * tpar + 512, 512)
                        gts = [tab_gate[tpar]] if tab_gate[tpar] else []
                        ev_tab = dma_unit("g", "g%d" % (1 + tpar), [
                            lambda g, j=j, blk=blk, cos_t=cos_t: g.dma_start(
                                out=cos_t, in_=cos_d[128 * j:128 * j + 128, 512 * blk:512 * blk + 512]),
                            lambda g, j=j, blk=blk, sin_t=sin_t: g.dma_start(
                                out=sin_t, in_=sin_d[128 * j:128 * j + 128, 512 * blk:512 * blk + 512]),
                        ], gates=gts)
                        ev_c = ev_s2 = ev_tab
                        for (w_sb, bias_col, out_base) in ((wq_sb, bqcol, Q_FM), (wk_sb, bkcol, K_FM)):
                            for (bi, m) in ((0, j), (1, j + 2)):
                                P.guard_write(bi)
                                ev_mm = None
                                for k in range(8):
                                    ev_mm = t_do(lambda t, k=k, m=m, bi=bi, w_sb=w_sb, xln_blk=xln_blk: t.matmul(
                                        out=bank32(bi), lhsT=w_sb[k][:, 128 * m:128 * m + 128],
                                        rhs=xln_blk[k], start=(k == 0), stop=(k == 7)),
                                        inc=(k == 7))
                                P.set_write(bi, ev_mm)
                            Sd.wait_all("v", [ev_c, ev_s2])
                            P.guard_read("v", 0)
                            P.guard_read("v", 1)
                            t1 = m32(SC_T12, 512)
                            t2 = m32(SC_T12 + 512, 512)
                            t3 = m32(SC_T12 + 1024, 512)
                            t4 = m32(SC_T12 + 1536, 512)
                            o1 = mega[:, out_base + 4096 * j + 512 * blk:
                                      out_base + 4096 * j + 512 * blk + 512]
                            o2 = mega[:, out_base + 4096 * (j + 2) + 512 * blk:
                                      out_base + 4096 * (j + 2) + 512 * blk + 512]
                            be = bias_col[:, j:j + 1]
                            bo_ = bias_col[:, j + 2:j + 3]
                            v_do(lambda v, cos_t=cos_t, t1=t1, be=be: v.scalar_tensor_tensor(
                                out=t1, in0=bank32(0), scalar=be, in1=cos_t, op0=ALU.add, op1=ALU.mult))
                            v_do(lambda v, sin_t=sin_t, t2=t2, bo_=bo_: v.scalar_tensor_tensor(
                                out=t2, in0=bank32(1), scalar=bo_, in1=sin_t, op0=ALU.add, op1=ALU.mult))
                            v_do(lambda v, sin_t=sin_t, t3=t3, be=be: v.scalar_tensor_tensor(
                                out=t3, in0=bank32(0), scalar=be, in1=sin_t, op0=ALU.add, op1=ALU.mult))
                            ev_t4 = v_do(lambda v, cos_t=cos_t, t4=t4, bo_=bo_: v.scalar_tensor_tensor(
                                out=t4, in0=bank32(1), scalar=bo_, in1=cos_t, op0=ALU.add, op1=ALU.mult), inc=True)
                            Sd.wait("v", ev_t4)
                            v_do(lambda v, o1=o1, t1=t1, t2=t2: v.tensor_tensor(out=o1, in0=t1, in1=t2, op=ALU.subtract))
                            ev_r2 = v_do(lambda v, o2=o2, t3=t3, t4=t4: v.tensor_tensor(out=o2, in0=t3, in1=t4, op=ALU.subtract), inc=True)
                            P.set_read(0, ev_r2)
                            P.set_read(1, ev_r2)
                            tab_gate[tpar] = ev_r2
                else:
                    Sd.wait_all("t", [evw_q, ev_init])
                    for m in range(4):
                        bi = m % 2
                        P.guard_write(bi)
                        ev_mm = None
                        for k in range(8):
                            ev_mm = t_do(lambda t, k=k, m=m, bi=bi, xln_blk=xln_blk: t.matmul(
                                out=bank32(bi), lhsT=wq_sb[k][:, 128 * m:128 * m + 128],
                                rhs=xln_blk[k], start=(k == 0), stop=(k == 7)), inc=(k == 7))
                        P.set_write(bi, ev_mm)
                        P.guard_read("v", bi)
                        o1 = mega[:, Q_FM + 4096 * m + 512 * blk: Q_FM + 4096 * m + 512 * blk + 512]
                        ev_e = v_do(lambda v, bi=bi, o1=o1: v.tensor_copy(o1, bank32(bi)), inc=True)
                        P.set_read(bi, ev_e)

                if not is_b:
                    Sd.wait("t", evw_v)
                for i in range(4):
                    tt = 4 * blk + i
                    bi = 4 + (i % 2)
                    P.guard_write(bi)
                    for k in range(8):
                        ev_mm = t_do(lambda t, k=k, i=i, bi=bi, wv_sb=wv_sb, xln_blk=xln_blk: t.matmul(
                            out=bank32(bi), lhsT=xln_blk[k][:, 128 * i:128 * i + 128],
                            rhs=wv_sb[k], start=(k == 0), stop=(is_b and k == 7)),
                            inc=(is_b and k == 7))
                    if not is_b:
                        ev_mm = t_do(lambda t, bi=bi: t.matmul(
                            out=bank32(bi), lhsT=e0b[:, :],
                            rhs=mega[:, BV_PAD: BV_PAD + 512],
                            start=False, stop=True), inc=True)
                    P.set_write(bi, ev_mm)
                    P.guard_read("v", bi)
                    Sd.wait_all("v", ev_v65ones)
                    ov = mega[:, R_V65 + 520 * tt: R_V65 + 520 * tt + 520].rearrange(
                        "p (h x) -> p h x", x=65)[:, :, 0:64]
                    ev_e = v_do(lambda v, bi=bi, ov=ov: v.tensor_copy(
                        ov, bank32(bi).rearrange("p (h x) -> p h x", h=8)), inc=True)
                    P.set_read(bi, ev_e)
                blk_gate[par] = tfront()

        def phase_attn(is_b, agin, cc_piece=None):
            phase_gate()
            q_base = Q_FM
            k_base = Q_FM if is_b else K_FM
            ev_stage_dma = {}
            bc_gate = {0: None, 1: None}
            evs_dma = []
            unit = 0
            for c in range(4):
                for qo in range(8):
                    kt_max = 32 if is_b else 4 * (qo + 1)
                    for b in range(2):
                        P.guard_write(b)
                    first = [True, True]
                    pb_ev_ctx = [None, None, None, None]
                    pending = []
                    ev_ctx_last = None

                    def emit_ctx(pend):
                        nonlocal ev_ctx_last
                        kt, c0, pb, ev_exp, ev_mask = pend
                        Sd.wait("t", ev_exp)
                        Sd.wait("t", ev_mask)
                        for h in range(2):
                            st = first[h]
                            first[h] = False
                            ev_ctx_last = t_do(lambda t, kt=kt, c0=c0, pb=pb, h=h, st=st,
                                               sp=(kt == kt_max - 1), c=c: t.matmul(
                                out=ps32[0:65, 512 * h + c0: 512 * h + 512],
                                lhsT=mega[:, R_V65 + 520 * kt + 65 * (2 * c + h):
                                          R_V65 + 520 * kt + 65 * (2 * c + h) + 65],
                                rhs=pb[:, 512 * h + c0: 512 * h + 512],
                                start=st, stop=sp), inc=(h == 1))
                        pb_ev_ctx[kt % 4] = ev_ctx_last

                    for kt in range(kt_max):
                        c0 = 0 if is_b else max(0, kt * 128 - 512 * qo)
                        pb = m16(P_BUF + 1024 * (kt % 4), 1024)
                        sp2 = kt % 3
                        sreg0 = 1024 + 1024 * sp2
                        ev_sm = None
                        for h in range(2):
                            ol = 64 * h
                            P.guard_write(2 + 2 * sp2 + h)
                            ev_sm = t_do(lambda t, kt=kt, c0=c0, h=h, ol=ol, c=c, qo=qo,
                                         sreg0=sreg0: t.matmul(
                                out=ps32[:, sreg0 + 512 * h + c0: sreg0 + 512 * h + 512],
                                lhsT=mega[ol:ol + 64, k_base + c * 4096 + 128 * kt:
                                          k_base + c * 4096 + 128 * kt + 128],
                                rhs=mega[ol:ol + 64, q_base + c * 4096 + 512 * qo + c0:
                                         q_base + c * 4096 + 512 * qo + 512],
                                start=True, stop=True), inc=(h == 1))
                        for h in range(2):
                            P.set_write(2 + 2 * sp2 + h, ev_sm)
                        Sd.wait("s", pb_ev_ctx[kt % 4])
                        P.guard_read("s", 2 + 2 * sp2)
                        P.guard_read("s", 3 + 2 * sp2)
                        ev_exp = s_do(lambda s, sreg0=sreg0, pb=pb: s.activation(
                            out=pb[:, 0:1024], in_=ps32[:, sreg0: sreg0 + 1024],
                            func=AF.Exp, scale=cA[:, :], bias=cB[:, :]), inc=True)
                        P.set_read(2 + 2 * sp2, ev_exp)
                        P.set_read(3 + 2 * sp2, ev_exp)
                        ev_mask = None
                        if (not is_b) and kt * 128 >= 512 * qo:
                            Sd.wait("v", ev_exp)
                            v_do(lambda v, c0=c0, pb=pb: v.tensor_tensor(
                                out=pb[:, c0:c0 + 128], in0=pb[:, c0:c0 + 128],
                                in1=trib[:, :], op=ALU.mult))
                            ev_mask = v_do(lambda v, c0=c0, pb=pb: v.tensor_tensor(
                                out=pb[:, 512 + c0:512 + c0 + 128],
                                in0=pb[:, 512 + c0:512 + c0 + 128],
                                in1=trib[:, :], op=ALU.mult), inc=True)
                        pending.append((kt, c0, pb, ev_exp, ev_mask))
                        if len(pending) == 3:
                            emit_ctx(pending.pop(0))
                            emit_ctx(pending.pop(0))
                    for pend in pending:
                        emit_ctx(pend)
                    pending = []
                    for b in range(2):
                        P.set_write(b, ev_ctx_last)
                    Sd.wait("v", ev_ctx_last)
                    ev_rc = v_do(lambda v: v.reciprocal(out=recip_f(),
                                                        in_=ps32[64:65, 0:1024]), inc=True)
                    Sd.wait("v", ev_rc)
                    ev_rcb = v_do(lambda v: v.tensor_copy(
                        mega[0:1, RECIP_PAD: RECIP_PAD + 1024], recip_f()), inc=True)
                    ev_sts = []
                    for h in range(2):
                        bci = 4 + h
                        P.guard_write(bci)
                        Sd.wait("t", ev_rcb)
                        ev_bc = t_do(lambda t, h=h, bci=bci: t.matmul(
                            out=bank32(bci), lhsT=e0b[:, :],
                            rhs=mega[:, RECIP_PAD + 512 * h: RECIP_PAD + 512 * h + 512],
                            start=True, stop=True), inc=True)
                        P.set_write(bci, ev_bc)
                        stg = m16(CTX_STG + 512 * (2 * (unit % 2) + h), 512)
                        Sd.wait("v", ev_stage_dma.get((unit % 2, h)))
                        P.guard_read("v", bci)
                        bc_sb = m32(SC_BC + 512 * h, 512)
                        Sd.wait("v", bc_gate[h])
                        ev_cp = v_do(lambda v, bci=bci, bc_sb=bc_sb: v.tensor_copy(
                            bc_sb[0:64, :], bank32(bci)[0:64, :]), inc=True)
                        P.set_read(bci, ev_cp)
                        Sd.wait("v", ev_cp)
                        ev_st = v_do(lambda v, h=h, stg=stg, bc_sb=bc_sb: v.tensor_tensor(
                            out=stg[0:64, :], in0=ps32[0:64, 512 * h: 512 * h + 512],
                            in1=bc_sb[0:64, :], op=ALU.mult), inc=True)
                        bc_gate[h] = ev_st
                        ev_sts.append(ev_st)
                        ev_d = y_dma(agin[64 * (2 * c + h): 64 * (2 * c + h) + 64,
                                          512 * qo: 512 * qo + 512],
                                     stg[0:64, :], ch="y%d" % (2 + unit % 2),
                                     gates=[ev_st])
                        ev_stage_dma[(unit % 2, h)] = ev_d
                        evs_dma.append(ev_d)
                    for b in range(2):
                        P.set_read(b, ev_sts[-1])
                    unit += 1
                if cc_piece is not None:
                    Sd.wait_all("g", evs_dma[-16:])
                    cc_piece(c)
            return evs_dma

        def xres_view():
            return mega[:, R_QK: R_QK + 2 * 16384].bitcast(F32)

        def phase_outproj(agout_h, w_dram, has_bias, ev_cc):
            phase_gate()
            if has_bias:
                Sd.wait("g", tfront())
                Sd.wait("g", sfront())
                Sd.wait("g", vfront())
                ev_z = g_other(lambda g: g.memset(mega[:, BO_PAD: BO_PAD + 1024], 0.0))
                Sd.wait("v", ev_z)
                ev_bp = v_do(lambda v: v.tensor_copy(mega[0:1, BO_PAD: BO_PAD + 1024],
                                                     biaso[0:1, :]), inc=True)
                Sd.wait("t", ev_bp)
            evw = load_w_kxm(w_dram, R_W, 1024, gates=[tfront()])
            wo_sb = [mega[:, R_W + 1024 * k: R_W + 1024 * k + 1024] for k in range(8)]
            ctxf = [mega[:, R_V65 + 2048 * k: R_V65 + 2048 * k + 2048] for k in range(8)]

            def load_ctx_k(g, k):
                row = 256 * (k % 4) + 128 * (k // 4)
                g.reg_alu(REG["tmp"], REG["own"], row * S, ALU.add)
                return g.dma_start(out=ctxf[k],
                                   in_=bass.AP(agout_h, REG["tmp"], [[S, 128], [1, 2048]]))

            ev_ctx = dma_unit("g", "g0",
                              [lambda g, k=k: load_ctx_k(g, k) for k in range(8)],
                              gates=[ev_cc, tfront()])
            if "opj" in dbg:
                Sd.wait("y", ev_ctx)
                barrier_into("y")
                dump("d_ctxf", dbg_dram("d_ctxf", [128, 16384], BF16), m16(R_V65, 16384))
                dump("d_agout", dbg_dram("d_agout", [1024, S], BF16), None,
                     dram_src=agout_h.ap())
            Sd.wait_all("t", [evw, ev_ctx])
            ev_add = None
            for tt in range(16):
                for n2 in range(2):
                    bi = (2 * tt + n2) % 4
                    P.guard_write(bi)
                    for k in range(8):
                        ev_mm = t_do(lambda t, k=k, tt=tt, n2=n2, bi=bi: t.matmul(
                            out=bank32(bi), lhsT=ctxf[k][:, 128 * tt:128 * tt + 128],
                            rhs=wo_sb[k][:, 512 * n2:512 * n2 + 512],
                            start=(k == 0), stop=(not has_bias and k == 7)),
                            inc=(not has_bias and k == 7))
                    if has_bias:
                        ev_mm = t_do(lambda t, n2=n2, bi=bi: t.matmul(
                            out=bank32(bi), lhsT=e0b[:, :],
                            rhs=mega[:, BO_PAD + 512 * n2: BO_PAD + 512 * n2 + 1024 - 512],
                            start=False, stop=True), inc=True)
                    P.set_write(bi, ev_mm)
                    P.guard_read("v", bi)
                    if "ydump" in dbg and tt == 0 and n2 == 0:
                        yscr = m16(P_BUF, 1024).bitcast(F32)
                        ev_yd = v_do(lambda v, bi=bi, yscr=yscr: v.tensor_copy(
                            yscr, bank32(bi)), inc=True)
                        Sd.wait("y", ev_yd)
                        dump("d_y0", dbg_dram("d_y0", [128, 512], F32), yscr)
                    xr = xres_view()
                    ev_add = v_do(lambda v, tt=tt, n2=n2, bi=bi, xr=xr: v.tensor_tensor(
                        out=xr[:, 1024 * tt + 512 * n2: 1024 * tt + 512 * n2 + 512],
                        in0=xr[:, 1024 * tt + 512 * n2: 1024 * tt + 512 * n2 + 512],
                        in1=bank32(bi), op=ALU.add), inc=True)
                    P.set_read(bi, ev_add)
            return ev_add


        def phase_ffn(half_done=None):
            phase_gate()
            xr = xres_view()
            ev_wd = g_dma(mega[:, R_WD: R_WD + 24576].rearrange("p (k m) -> p k m", k=24),
                          wdT_d.rearrange("(k p) m -> p k m", p=128),
                          gates=[tfront(), vfront()])
            wd_sb = [mega[:, R_WD + 1024 * hc: R_WD + 1024 * hc + 1024] for hc in range(24)]
            ev_r = None
            for half in range(2):
                evs = []
                for i in range(8):
                    tt = 8 * half + i
                    ov = mega[:, R_XLN: R_XLN + 8192].rearrange(
                        "p (c x) -> p c x", x=1024)[:, :, 128 * i:128 * i + 128]
                    evs.append(ln_core(xr[:, 1024 * tt: 1024 * tt + 1024], tt, ov,
                                       stage_base=R_W + 5120, junk_base=R_W + 7168)[0])
                xlnh = [mega[:, R_XLN + 1024 * k: R_XLN + 1024 * k + 1024] for k in range(8)]
                Sd.wait("t", evs[-1])
                hv_evs = {0: vfront(), 1: vfront()}
                gs_gate = {0: vfront(), 1: vfront()}
                ev_h = None
                for hc in range(24):
                    par = hc % 2
                    wbase = R_W + 2048 * par
                    wgv = mega[:, wbase: wbase + 1024].rearrange("p (k m) -> p k m", k=8)
                    wuv = mega[:, wbase + 1024: wbase + 2048].rearrange("p (k m) -> p k m", k=8)
                    gts = [hv_evs[par]] if hv_evs[par] else []
                    ev_wu2 = dma_unit("g", "g%d" % (1 + par), [
                        lambda g, hc=hc, wgv=wgv: g.dma_start(
                            out=wgv, in_=wgT_d[:, 128 * hc:128 * hc + 128].rearrange(
                                "(k p) m -> p k m", p=128)),
                        lambda g, hc=hc, wuv=wuv: g.dma_start(
                            out=wuv, in_=wuT_d[:, 128 * hc:128 * hc + 128].rearrange(
                                "(k p) m -> p k m", p=128)),
                    ], gates=gts)
                    ev_g1 = ev_g2 = ev_wu2
                    wg_k = [mega[:, wbase + 128 * k: wbase + 128 * k + 128] for k in range(8)]
                    wu_k = [mega[:, wbase + 1024 + 128 * k: wbase + 1024 + 128 * k + 128] for k in range(8)]
                    Sd.wait_all("t", [ev_g1, ev_g2])
                    for tb in range(2):
                        big = (2 * tb) % 4
                        biu = big + 1
                        P.guard_write(big)
                        ev_gm = None
                        for k in range(8):
                            ev_gm = t_do(lambda t, k=k, tb=tb, big=big, wg_k=wg_k, xlnh=xlnh: t.matmul(
                                out=bank32(big), lhsT=wg_k[k],
                                rhs=xlnh[k][:, 512 * tb:512 * tb + 512],
                                start=(k == 0), stop=(k == 7)), inc=(k == 7))
                        P.set_write(big, ev_gm)
                        P.guard_write(biu)
                        ev_um = None
                        for k in range(8):
                            ev_um = t_do(lambda t, k=k, tb=tb, biu=biu, wu_k=wu_k, xlnh=xlnh: t.matmul(
                                out=bank32(biu), lhsT=wu_k[k],
                                rhs=xlnh[k][:, 512 * tb:512 * tb + 512],
                                start=(k == 0), stop=(k == 7)), inc=(k == 7))
                        P.set_write(biu, ev_um)
                        P.guard_read("s", big)
                        gpar = (2 * hc + tb) % 2
                        gs = m16(R_W + 4096 + 512 * gpar, 512)
                        Sd.wait("s", gs_gate[gpar])
                        ev_si = s_do(lambda s, big=big, gs=gs: s.activation(
                            out=gs, in_=bank32(big), func=AF.Silu, scale=cC[:, :]), inc=True)
                        P.set_read(big, ev_si)
                        hv = mega[:, R_V65 + 1024 * hc + 512 * tb:
                                  R_V65 + 1024 * hc + 512 * tb + 512]
                        Sd.wait("v", ev_si)
                        P.guard_read("v", biu)
                        ev_h = v_do(lambda v, biu=biu, gs=gs, hv=hv: v.tensor_tensor(
                            out=hv, in0=gs, in1=bank32(biu), op=ALU.mult), inc=True)
                        P.set_read(biu, ev_h)
                        gs_gate[gpar] = ev_h
                    hv_evs[par] = ev_h
                Sd.wait("t", ev_wd)
                Sd.wait("t", ev_h)
                for i in range(8):
                    tt = 8 * half + i
                    for n2 in range(2):
                        bi = (2 * i + n2) % 4
                        P.guard_write(bi)
                        ev_dm = None
                        for hc in range(24):
                            ev_dm = t_do(lambda t, hc=hc, i=i, n2=n2, bi=bi: t.matmul(
                                out=bank32(bi),
                                lhsT=mega[:, R_V65 + 1024 * hc + 128 * i:
                                          R_V65 + 1024 * hc + 128 * i + 128],
                                rhs=wd_sb[hc][:, 512 * n2:512 * n2 + 512],
                                start=(hc == 0), stop=(hc == 23)), inc=(hc == 23))
                        P.set_write(bi, ev_dm)
                        P.guard_read("v", bi)
                        ev_r = v_do(lambda v, tt=tt, n2=n2, bi=bi, xr=xr: v.scalar_tensor_tensor(
                            out=xr[:, 1024 * tt + 512 * n2: 1024 * tt + 512 * n2 + 512],
                            in0=bank32(bi), scalar=1.0 / (QS * QS),
                            in1=xr[:, 1024 * tt + 512 * n2: 1024 * tt + 512 * n2 + 512],
                            op0=ALU.mult, op1=ALU.add), inc=True)
                        P.set_read(bi, ev_r)
                if half_done is not None:
                    half_done(half, ev_r)
            return ev_r

        def dump(name, dram_ap, sb_view, dram_src=None):
            if dram_src is not None:
                ev = g_dma(dram_ap, dram_src)
                Sd.wait("g", ev)
            else:
                ev = y_dma(dram_ap, sb_view)
                Sd.wait("y", ev)

        def barrier_into(eng):
            for sem, c in Sd.counts.items():
                Sd.wait(eng, (sem, c))

        def main_tail(cc_evs_x):
            xr = xres_view()
            Sd.wait_all("y", cc_evs_x)
            phase_proj(is_b=True)
            if "stop_qkvb" in dbg:
                barrier_into("y")
                dump("d_qkvb", dbg_dram("d_qkvb", [512, S], BF16).rearrange(
                    "(a p) c -> p a c", p=128),
                    m16(Q_FM, 16384).rearrange("p (a c) -> p a c", a=4))
                return
            cc_evs_b = []

            def cc_piece_b(c):
                ev = Sd.do("g", lambda g, c=c: g.collective_compute(
                    "AllGather", ALU.bypass, replica_groups=groups,
                    ins=[agin_b[128 * c:128 * c + 128, :].opt()],
                    outs=[agout_b[256 * c:256 * c + 256, :].opt()]), "cc", 1)
                cc_evs_b.append(ev)

            phase_attn(is_b=True, agin=agin_b, cc_piece=cc_piece_b)
            ev_cc_b = cc_evs_b[-1]
            ev_xr2 = y_dma(xr.rearrange("p (a c) -> p a c", a=16),
                           agin_x.rearrange("(a p) c -> p a c", p=128),
                           gates=[tfront()])
            Sd.wait("v", ev_xr2)
            phase_outproj(agout_b_h, Wob_d, has_bias=False, ev_cc=ev_cc_b)
            if "x3" in dbg:
                barrier_into("y")
                dump("d_x3", dbg_dram("d_x3", [T_OWN, D], F32).rearrange("(a p) c -> p a c", p=128),
                     xr.rearrange("p (a c) -> p a c", a=16))
            if "stop_x3" in dbg:
                return
            phase_ffn()
            ev_out = y_dma(y_out.rearrange("(a p) c -> p a c", p=128),
                           xr.rearrange("p (a c) -> p a c", a=16),
                           gates=[vfront()])
            Sd.wait("y", ev_out)

        phase_proj(is_b=False)
        if "qkv" in dbg:
            barrier_into("y")
            dump("d_q", dbg_dram("d_q", [512, S], BF16).rearrange("(a p) c -> p a c", p=128),
                 m16(Q_FM, 16384).rearrange("p (a c) -> p a c", a=4))
            dump("d_k", dbg_dram("d_k", [512, S], BF16).rearrange("(a p) c -> p a c", p=128),
                 m16(K_FM, 16384).rearrange("p (a c) -> p a c", a=4))
            dump("d_v", dbg_dram("d_v", [128, 32 * 520], BF16),
                 m16(R_V65, 16640))
        if "stop_qkv" in dbg:
            barrier_into("y")
        elif "stop_ctxa" in dbg:
            evs_a = phase_attn(is_b=False, agin=agin_a)
            Sd.wait_all("g", evs_a)
            barrier_into("y")
            dump("d_ctxa", dbg_dram("d_ctxa", [512, S], BF16), None, dram_src=agin_a)
        else:
            cc_evs_a = []

            def cc_piece_a(c):
                ev = Sd.do("g", lambda g, c=c: g.collective_compute(
                    "AllGather", ALU.bypass, replica_groups=groups,
                    ins=[agin_a[128 * c:128 * c + 128, :].opt()],
                    outs=[agout_a[256 * c:256 * c + 256, :].opt()]), "cc", 1)
                cc_evs_a.append(ev)

            phase_attn(is_b=False, agin=agin_a, cc_piece=cc_piece_a)
            ev_cc_a = cc_evs_a[-1]
            xr = xres_view()
            ev_xr = y_dma(xr.rearrange("p (a c) -> p a c", a=16),
                          xown.rearrange("(a p) c -> p a c", p=128),
                          gates=[tfront()])
            Sd.wait("v", ev_xr)
            phase_outproj(agout_a_h, Woa_d, has_bias=True, ev_cc=ev_cc_a)
            if "x1" in dbg:
                barrier_into("y")
                dump("d_x1", dbg_dram("d_x1", [T_OWN, D], F32).rearrange("(a p) c -> p a c", p=128),
                     xr.rearrange("p (a c) -> p a c", a=16))
            if "stop_x1" in dbg:
                return_early = True
            else:
                return_early = False
            cc_evs_x = []

            def x_half_done(half, ev_r):
                ev_w = g_dma(agin_x[1024 * half:1024 * half + 1024, :].rearrange(
                    "(a p) c -> p a c", p=128),
                    xres_view()[:, 8192 * half: 8192 * half + 8192].rearrange(
                    "p (a c) -> p a c", a=8), gates=[ev_r])
                Sd.wait("g", ev_w)
                ev = Sd.do("g", lambda g, half=half: g.collective_compute(
                    "AllGather", ALU.bypass, replica_groups=groups,
                    ins=[agin_x[1024 * half:1024 * half + 1024, :].opt()],
                    outs=[agout_x[2048 * half:2048 * half + 2048, :].opt()]), "cc", 1)
                cc_evs_x.append(ev)

            if not return_early:
                phase_ffn(half_done=x_half_done)
                if "x2" in dbg:
                    barrier_into("y")
                    dump("d_x2", dbg_dram("d_x2", [T_OWN, D], F32).rearrange("(a p) c -> p a c", p=128),
                         xr.rearrange("p (a c) -> p a c", a=16))
            if return_early or "stop_x2" in dbg:
                pass
            else:
                main_tail(cc_evs_x)
        if False:
            pass

        with nc.Block() as block:
            @block.tensor
            def _(t):
                Sd.replay("t", t, sems)

            @block.vector
            def _(v):
                Sd.replay("v", v, sems)

            @block.scalar
            def _(s):
                Sd.replay("s", s, sems)

            @block.gpsimd
            def _(g):
                Sd.replay("g", g, sems)

            @block.sync
            def _(y):
                Sd.replay("y", y, sems)

        ninstr = {e: len(Sd.cmds[e]) for e in Sd.ENGS}
    return nc, ninstr


def host_prep(inputs):
    Wq, bq = np.asarray(inputs["L0_attn_a_Wq"]), np.asarray(inputs["L0_attn_a_bq"])
    Wk, bk = np.asarray(inputs["L0_attn_a_Wk"]), np.asarray(inputs["L0_attn_a_bk"])
    Wv, bv = np.asarray(inputs["L0_attn_a_Wv"]), np.asarray(inputs["L0_attn_a_bv"])
    Wo, bo = np.asarray(inputs["L0_attn_a_Wo"]), np.asarray(inputs["L0_attn_a_bo"])
    Wqkv = np.asarray(inputs["L0_attn_b_W_qkv"])
    Wob = np.asarray(inputs["L0_attn_b_W_o"])
    wg = np.asarray(inputs["L0_ffn_w_gate"])
    wu = np.asarray(inputs["L0_ffn_w_up"])
    wd = np.asarray(inputs["L0_ffn_w_down"])
    x0 = np.asarray(inputs["L0_attn_a_x"])

    j = np.arange(512, dtype=np.float32)
    freq_cis = np.power(np.float32(10000.0), np.float32(-2.0) * j / np.float32(D))
    pos = np.arange(S, dtype=np.float32)
    freq = np.outer(pos, freq_cis).astype(np.float32)
    cosf = np.cos(freq).astype(np.float32)
    sinf = np.sin(freq).astype(np.float32)

    perm = np.empty(1024, dtype=np.int64)
    for pp in (0, 1):
        for d_ in range(512):
            r = 512 * pp + d_
            perm[r] = (256 * pp + d_) if d_ < 256 else (512 + 256 * pp + (d_ - 256))
    Wo_perm = np.ascontiguousarray(Wo[perm, :])

    tri = np.triu(np.ones((128, 128), np.float32)).astype(ml_dtypes.bfloat16)
    ident = np.eye(128, dtype=ml_dtypes.bfloat16)
    e0 = np.zeros((128, 128), dtype=ml_dtypes.bfloat16)
    e0[0, :] = 1
    wgT = np.ascontiguousarray(wg.T)
    wuT = np.ascontiguousarray(wu.T)
    wdT = np.ascontiguousarray(wd.T)
    WobT = np.ascontiguousarray(Wob.T)

    per_core = []
    for core in range(8):
        s, p = core // 2, core % 2
        sl = slice(512 * p, 512 * p + 512)
        Wq_c = np.ascontiguousarray(np.concatenate(
            [Wq[:, sl][:, 0::2], Wq[:, sl][:, 1::2]], axis=1))
        Wk_c = np.ascontiguousarray(np.concatenate(
            [Wk[:, sl][:, 0::2], Wk[:, sl][:, 1::2]], axis=1))
        bq_cc = np.concatenate([bq[sl][0::2], bq[sl][1::2]]).astype(np.float32)
        bk_cc = np.concatenate([bk[sl][0::2], bk[sl][1::2]]).astype(np.float32)
        bqcol = np.ascontiguousarray(bq_cc.reshape(4, 128).T)
        bkcol = np.ascontiguousarray(bk_cc.reshape(4, 128).T)
        gheads = [4 * p + l for l in range(4)] + [8 + 4 * p + l for l in range(4)]
        vcols = np.concatenate([np.arange(64 * g, 64 * g + 64) for g in gheads])
        Wv_c = np.ascontiguousarray(Wv[:, vcols])
        bv_c = np.ascontiguousarray(bv[vcols])[None, :]
        jsl = slice(256 * p, 256 * p + 256)
        cos_c = np.ascontiguousarray(cosf[:, jsl].T)
        sin_c = np.ascontiguousarray(sinf[:, jsl].T)
        Wqkv_c = np.ascontiguousarray(Wqkv[512 * p: 512 * p + 512, :].T)
        per_core.append({
            "xseq": np.ascontiguousarray(x0[s]),
            "xown": np.ascontiguousarray(x0[s][2048 * p: 2048 * p + 2048]),
            "Wq_c": Wq_c, "Wk_c": Wk_c, "Wv_c": Wv_c,
            "bqcol": bqcol, "bkcol": bkcol,
            "bv_c": np.ascontiguousarray(bv_c),
            "e0": e0,
            "Wo_a": Wo_perm, "bo_a": np.ascontiguousarray(bo[None, :]),
            "cos_c": cos_c, "sin_c": sin_c,
            "Wqkv_c": Wqkv_c, "Wo_b": WobT,
            "wgT": wgT, "wuT": wuT, "wdT": wdT,
            "tri": tri, "ident": ident,
        })
    return per_core


def kernel(**inputs):
    from concourse.bass_utils import run_bass_kernel_spmd
    per_core = host_prep(inputs)
    nc, _ = build()
    res = run_bass_kernel_spmd(nc, per_core, core_ids=list(range(8)))
    out = np.zeros((B, S, D), np.float32)
    for core in range(8):
        s, p = core // 2, core % 2
        out[s, 2048 * p: 2048 * p + 2048] = res.results[core]["y_out"]
    return out


# version 09
# speedup vs baseline: 1.0955x; 1.0225x over previous
import numpy as np
import ml_dtypes
import concourse.bass as bass
import concourse.mybir as mybir

F32 = mybir.dt.float32
BF16 = mybir.dt.bfloat16
I32 = mybir.dt.int32
AF = mybir.ActivationFunctionType
ALU = mybir.AluOpType

B, S, D, NH, DH, HID = 4, 4096, 1024, 16, 64, 3072
HL = 8
T_OWN = 2048
SHIFT = 12.0
QS = 127.0

R_XLN = 0
R_QK = 8192
R_V65 = 40960
R_CS = 57600
R_W = 65792
R_WD = 78080
W16 = 102656

Q_FM = R_QK
K_FM = R_QK + 16384
P_BUF = R_CS
CTX_STG = R_CS + 4096
RECIP_PAD = R_CS + 6144
BV_PAD = R_CS + 7168
BO_PAD = R_CS
XT_STG = R_CS
JUNK_A = R_CS + 4096

SC_XT = 0
SC_T12 = 2048
SC_TAB = 8192
SC_BC = 11264


class Sched:
    ENGS = ("t", "v", "s", "g", "y")

    def __init__(self):
        self.cmds = {e: [] for e in self.ENGS}
        self.counts = {}
        self.lastwait = {}

    def do(self, eng, fn, sem=None, amt=1):
        if sem is None:
            self.cmds[eng].append(("plain", fn))
            return None
        c = self.counts.get(sem, 0) + amt
        self.counts[sem] = c
        self.cmds[eng].append(("inc", fn, sem, amt))
        return (sem, c)

    def wait(self, eng, ev):
        if ev is None:
            return
        sem, val = ev
        key = (eng, sem)
        if self.lastwait.get(key, -1) >= val:
            return
        self.lastwait[key] = val
        self.cmds[eng].append(("wait", sem, val))

    def wait_all(self, eng, evs):
        for ev in evs:
            self.wait(eng, ev)

    def frontier(self):
        return [(s, c) for s, c in self.counts.items()]

    def replay(self, eng, handle, sems):
        for cmd in self.cmds[eng]:
            if cmd[0] == "plain":
                cmd[1](handle)
            elif cmd[0] == "inc":
                cmd[1](handle).then_inc(sems[cmd[2]], cmd[3])
            else:
                handle.wait_ge(sems[cmd[1]], cmd[2])


class Psum:
    def __init__(self, sched):
        self.S = sched
        self.write_ev = [None] * 8
        self.read_ev = [None] * 8

    def guard_write(self, bank):
        self.S.wait("t", self.read_ev[bank])

    def set_write(self, bank, ev):
        self.write_ev[bank] = ev

    def guard_read(self, eng, bank):
        self.S.wait(eng, self.write_ev[bank])

    def set_read(self, bank, ev):
        self.read_ev[bank] = ev


def build(debug=(), n_pairs=4):
    nc = bass.Bass("TRN2", target_bir_lowering=False)
    dbg = set(debug)

    xseq = nc.dram_tensor("xseq", [S, D], F32, kind="ExternalInput").ap()
    xown = nc.dram_tensor("xown", [T_OWN, D], F32, kind="ExternalInput").ap()
    Wq_d = nc.dram_tensor("Wq_c", [D, 512], F32, kind="ExternalInput").ap()
    Wk_d = nc.dram_tensor("Wk_c", [D, 512], F32, kind="ExternalInput").ap()
    Wv_d = nc.dram_tensor("Wv_c", [D, 512], F32, kind="ExternalInput").ap()
    bqc_d = nc.dram_tensor("bqcol", [128, 4], F32, kind="ExternalInput").ap()
    bkc_d = nc.dram_tensor("bkcol", [128, 4], F32, kind="ExternalInput").ap()
    bv_d = nc.dram_tensor("bv_c", [1, 512], F32, kind="ExternalInput").ap()
    e0_d = nc.dram_tensor("e0", [128, 128], BF16, kind="ExternalInput").ap()
    Woa_d = nc.dram_tensor("Wo_a", [D, D], F32, kind="ExternalInput").ap()
    boa_d = nc.dram_tensor("bo_a", [1, D], F32, kind="ExternalInput").ap()
    cos_d = nc.dram_tensor("cos_c", [256, S], F32, kind="ExternalInput").ap()
    sin_d = nc.dram_tensor("sin_c", [256, S], F32, kind="ExternalInput").ap()
    Wqkv_d = nc.dram_tensor("Wqkv_c", [D, 512], F32, kind="ExternalInput").ap()
    Wob_d = nc.dram_tensor("Wo_b", [D, D], F32, kind="ExternalInput").ap()
    wgT_d = nc.dram_tensor("wgT", [D, HID], I32, kind="ExternalInput").ap()
    wuT_d = nc.dram_tensor("wuT", [D, HID], I32, kind="ExternalInput").ap()
    wdT_d = nc.dram_tensor("wdT", [HID, D], I32, kind="ExternalInput").ap()
    tri_d = nc.dram_tensor("tri", [128, 128], BF16, kind="ExternalInput").ap()
    id_d = nc.dram_tensor("ident", [128, 128], BF16, kind="ExternalInput").ap()

    y_out = nc.dram_tensor("y_out", [T_OWN, D], F32, kind="ExternalOutput").ap()

    agin_a_h = nc.dram_tensor("agin_a", [512, S], BF16)
    agout_a_h = nc.dram_tensor("agout_a", [1024, S], BF16)
    agin_x_h = nc.dram_tensor("agin_x", [T_OWN, D], F32)
    agin_xb_h = nc.dram_tensor("agin_xb", [T_OWN, D], BF16)
    agout_xb_h = nc.dram_tensor("agout_xb", [S, D], BF16)
    agin_b_h = nc.dram_tensor("agin_b", [512, S], BF16)
    agout_b_h = nc.dram_tensor("agout_b", [1024, S], BF16)
    agin_a, agout_a = agin_a_h.ap(), agout_a_h.ap()
    agin_x = agin_x_h.ap()
    agin_xb, agout_xb = agin_xb_h.ap(), agout_xb_h.ap()
    agin_b, agout_b = agin_b_h.ap(), agout_b_h.ap()
    groups = [[2 * i, 2 * i + 1] for i in range(n_pairs)]

    dbg_outs = {}

    def dbg_dram(name, shape, dt):
        if name not in dbg_outs:
            dbg_outs[name] = nc.dram_tensor(name, shape, dt, kind="ExternalOutput").ap()
        return dbg_outs[name]

    from contextlib import ExitStack
    with ExitStack() as ctx:
        E = ctx.enter_context
        mega = E(nc.sbuf_tensor("mega", [128, W16], BF16))
        idb = E(nc.sbuf_tensor("idb", [128, 128], BF16))
        trib = E(nc.sbuf_tensor("trib", [128, 128], BF16))
        ones1 = E(nc.sbuf_tensor("ones1", [1, 512], BF16))
        ones1f = E(nc.sbuf_tensor("ones1f", [1, 128], F32))
        bqcol = E(nc.sbuf_tensor("bqcol_s", [128, 4], F32))
        bkcol = E(nc.sbuf_tensor("bkcol_s", [128, 4], F32))
        biaso = E(nc.sbuf_tensor("biaso", [1, 1024], BF16))
        e0b = E(nc.sbuf_tensor("e0b", [128, 128], BF16))
        cA = E(nc.sbuf_tensor("cA", [128, 1], F32))
        cB = E(nc.sbuf_tensor("cB", [128, 1], F32))
        cC = E(nc.sbuf_tensor("cC", [128, 1], F32))
        cD = E(nc.sbuf_tensor("cD", [128, 1], F32))
        cE = E(nc.sbuf_tensor("cE", [128, 1], F32))
        stats = E(nc.sbuf_tensor("stats", [128, 32], F32))
        ps32 = E(nc.psum_tensor("ps32", [128, 4096], F32))

        sems = {n: E(nc.semaphore("sem_" + n))
                for n in ("t", "v", "s", "cc", "gx",
                          "y0", "y1", "y2", "y3", "y4",
                          "g0", "g1", "g2", "g3")}

        Sd = Sched()
        P = Psum(Sd)
        REG = {}

        def m16(c0, cols):
            return mega[:, c0:c0 + cols]

        def m32(w0, words):
            return mega[:, R_WD + 2 * w0: R_WD + 2 * (w0 + words)].bitcast(F32)

        def bank32(b, n=512):
            return ps32[:, 512 * b: 512 * b + n]

        def recip_f():
            return m32(SC_XT, 1024)[0:1, :]

        def t_do(fn, inc=False):
            return Sd.do("t", fn, "t" if inc else None)

        def v_do(fn, inc=False):
            return Sd.do("v", fn, "v" if inc else None)

        def s_do(fn, inc=False):
            return Sd.do("s", fn, "s" if inc else None)

        chan_last = {}

        def dma_unit(eng, ch, transfers, gates=()):
            Sd.wait(eng, (ch, chan_last.get(ch, 0)))
            for gev in gates:
                Sd.wait(eng, gev)
            ev = None
            for fn in transfers:
                ev = Sd.do(eng, fn, ch, 16)
            chan_last[ch] = ev[1]
            return ev

        def g_dma(out, in_, ch="g0", gates=()):
            return dma_unit("g", ch, [lambda g: g.dma_start(out=out, in_=in_)], gates)

        def y_dma(out, in_, ch="y4", gates=()):
            return dma_unit("y", ch, [lambda y: y.dma_start(out=out, in_=in_)], gates)

        def g_other(fn):
            return Sd.do("g", fn, "gx", 1)

        def tfront():
            return ("t", Sd.counts.get("t", 0))

        def vfront():
            return ("v", Sd.counts.get("v", 0))

        def sfront():
            return ("s", Sd.counts.get("s", 0))

        def phase_gate():
            tf, vf, sf = tfront(), vfront(), sfront()
            Sd.wait("t", vf)
            Sd.wait("t", sf)
            Sd.wait("v", tf)
            Sd.wait("v", sf)
            Sd.wait("s", tf)
            Sd.wait("s", vf)

        def setup_regs(g):
            REG["own"] = g.alloc_register("own_off")
            REG["tmp"] = g.alloc_register("tmp_off")
            g.reg_mov(REG["own"], g.partition_id())
            g.reg_alu(REG["own"], REG["own"], 1, ALU.bitwise_and)
            g.reg_alu(REG["own"], REG["own"], 2048, ALU.mult)

        Sd.do("g", setup_regs)
        g_other(lambda g: g.memset(ones1[:, :], 1.0))
        g_other(lambda g: g.memset(ones1f[:, :], 1.0))
        g_other(lambda g: g.memset(cA[:, :], 0.125))
        g_other(lambda g: g.memset(cB[:, :], -SHIFT))
        g_other(lambda g: g.memset(cC[:, :], 1.0 / QS))
        g_other(lambda g: g.memset(cD[:, :], 1e-5))
        ev_init = g_other(lambda g: g.memset(cE[:, :], 1.0))
        g_dma(idb[:, :], id_d)
        g_dma(trib[:, :], tri_d)
        g_dma(e0b[:, :], e0_d)
        g_dma(bqcol[:, :], bqc_d)
        g_dma(bkcol[:, :], bkc_d)
        ev_rpz = g_other(lambda g: g.memset(mega[:, RECIP_PAD: RECIP_PAD + 1024], 0.0))
        ev_bvz = g_other(lambda g: g.memset(mega[:, BV_PAD: BV_PAD + 512], 0.0))
        Sd.wait("g", ev_bvz)
        g_dma(mega[0:1, BV_PAD: BV_PAD + 512], bv_d)
        ev_bias = g_dma(biaso[:, :], boa_d)
        Sd.wait("v", ev_rpz)

        for _e in ("t", "v", "s"):
            Sd.wait(_e, ev_init)
            Sd.wait(_e, ev_bias)

        v65_ones = mega[:, R_V65: R_V65 + 32 * 520].rearrange(
            "p (kt h x) -> p kt h x", h=8, x=65)[:, :, :, 64:65]
        ev_v65ones = [g_other(lambda g: g.memset(v65_ones, 1.0))]

        def load_w_kxm(dram, c0, ncols, nk=8, gates=()):
            out = mega[:, c0: c0 + nk * ncols].rearrange("p (k m) -> p k m", k=nk)
            return g_dma(out, dram.rearrange("(k p) m -> p k m", p=128), gates=gates)

        def ln_core(x_ap, tt, out_view, stage_base=XT_STG, junk_base=JUNK_A):
            sb8 = 8 * (tt % 4)
            st_sum = stats[:, sb8 + 0:sb8 + 1]
            st_sq = stats[:, sb8 + 1:sb8 + 2]
            st_mean = stats[:, sb8 + 2:sb8 + 3]
            st_var = stats[:, sb8 + 3:sb8 + 4]
            st_inv = stats[:, sb8 + 4:sb8 + 5]
            junk = m16(junk_base, 1024)
            junk2 = m16(junk_base + 1024, 1024)
            v_do(lambda v: v.tensor_scalar(out=junk[:, :], in0=x_ap, scalar1=1.0,
                                           scalar2=0.0, op0=ALU.mult, op1=ALU.add,
                                           accum_out=st_sum))
            ev_acc = v_do(lambda v: v.scalar_tensor_tensor(out=junk2[:, :], in0=x_ap, scalar=1.0,
                                                           in1=x_ap, op0=ALU.mult, op1=ALU.mult,
                                                           accum_out=st_sq), inc=True)
            Sd.wait("v", ev_acc)
            ev_m = v_do(lambda v: v.tensor_scalar(out=st_mean, in0=st_sum, scalar1=1.0 / D,
                                                  scalar2=0.0, op0=ALU.mult, op1=ALU.add), inc=True)
            Sd.wait("v", ev_m)
            ev_m2 = v_do(lambda v: v.tensor_tensor(out=st_var, in0=st_mean, in1=st_mean,
                                                   op=ALU.mult), inc=True)
            Sd.wait("v", ev_m2)
            ev_v = v_do(lambda v: v.tensor_scalar(out=st_var, in0=st_sq, scalar1=1.0 / D,
                                                  scalar2=st_var, op0=ALU.mult,
                                                  op1=ALU.subtract), inc=True)
            Sd.wait("s", ev_v)
            st_sd = stats[:, sb8 + 5:sb8 + 6]
            ev_sd = s_do(lambda s: s.activation(out=st_sd, in_=st_var, func=AF.Sqrt,
                                                scale=cE[:, :], bias=cD[:, :]), inc=True)
            Sd.wait("v", ev_sd)
            ev_rcp = v_do(lambda v: v.reciprocal(out=st_inv, in_=st_sd), inc=True)
            Sd.wait("v", ev_rcp)
            xln_tok = m16(stage_base + 1024 * (tt % 2), 1024)
            ev_app = v_do(lambda v: v.tensor_scalar(out=xln_tok, in0=x_ap, scalar1=st_mean,
                                                    scalar2=st_inv, op0=ALU.subtract,
                                                    op1=ALU.mult), inc=True)
            bank = 6 + (tt % 2)
            bcols = ps32[:, 3072 + 512 * (tt % 2): 3072 + 512 * (tt % 2) + 512].bitcast(BF16)
            Sd.wait("t", ev_app)
            P.guard_write(bank)
            ev_tr = None
            for c in range(8):
                ev_tr = t_do(lambda t, c=c, bcols=bcols, xln_tok=xln_tok: t.transpose(
                    out=bcols[:, 128 * c: 128 * c + 128],
                    in_=xln_tok[:, 128 * c: 128 * c + 128],
                    identity=idb[:, :]), inc=(c == 7))
            P.set_write(bank, ev_tr)
            P.guard_read("v", bank)
            ev_ev = v_do(lambda v, out_view=out_view, bcols=bcols: v.tensor_copy(
                out_view, bcols.rearrange("p (c x) -> p c x", c=8)), inc=True)
            P.set_read(bank, ev_ev)
            return ev_ev, ev_app

        xt_gate = {0: None, 1: None}

        def ln_tile_dram(src_rows, tt, out_view, bf=False):
            par = tt % 2
            if bf:
                xt = m32(SC_XT + 1024 * par, 512).bitcast(BF16)
            else:
                xt = m32(SC_XT + 1024 * par, 1024)
            g = [xt_gate[par]] if xt_gate[par] else []
            ld = y_dma(xt, src_rows, ch="y%d" % par, gates=g)
            Sd.wait("v", ld)
            ev_ev, ev_app = ln_core(xt, tt, out_view)
            xt_gate[par] = ev_app
            return ev_ev

        def phase_proj(is_b):
            phase_gate()
            blk_gate = {0: None, 1: None}
            tab_gate = {0: None, 1: None}
            if is_b:
                evw_q = load_w_kxm(Wqkv_d, R_W, 512)
                evw_k = evw_v = None
                wq_sb = [mega[:, R_W + 512 * k: R_W + 512 * k + 512] for k in range(8)]
                wk_sb = wv_sb = wq_sb
            else:
                evw_q = load_w_kxm(Wq_d, R_W, 512)
                evw_k = load_w_kxm(Wk_d, R_W + 4096, 512)
                evw_v = load_w_kxm(Wv_d, R_W + 8192, 512)
                wq_sb = [mega[:, R_W + 512 * k: R_W + 512 * k + 512] for k in range(8)]
                wk_sb = [mega[:, R_W + 4096 + 512 * k: R_W + 4096 + 512 * k + 512] for k in range(8)]
                wv_sb = [mega[:, R_W + 8192 + 512 * k: R_W + 8192 + 512 * k + 512] for k in range(8)]

            for blk in range(8):
                par = blk % 2
                xb_base = R_XLN + 4096 * par
                Sd.wait("v", blk_gate[par])
                for i in range(4):
                    tt = 4 * blk + i
                    ov = mega[:, xb_base: xb_base + 4096].rearrange(
                        "p (c x) -> p c x", x=512)[:, :, 128 * i:128 * i + 128]
                    if is_b:
                        g0 = 128 * tt
                        row = 2048 * ((g0 % 2048) // 1024) + 1024 * (g0 // 2048) + (g0 % 1024)
                        src = agout_xb[row:row + 128, :]
                        ev_ln = ln_tile_dram(src, tt, ov, bf=True)
                    else:
                        src = xseq[128 * tt:128 * tt + 128, :]
                        ev_ln = ln_tile_dram(src, tt, ov)
                xln_blk = [mega[:, xb_base + 512 * k: xb_base + 512 * k + 512] for k in range(8)]
                Sd.wait("t", ev_ln)

                if not is_b:
                    Sd.wait_all("t", [evw_q, evw_k, ev_bias, ev_init])
                    for j in range(2):
                        tpar = (2 * blk + j) % 2
                        cos_t = m32(SC_TAB + 1024 * tpar, 512)
                        sin_t = m32(SC_TAB + 1024 * tpar + 512, 512)
                        gts = [tab_gate[tpar]] if tab_gate[tpar] else []
                        ev_tab = dma_unit("g", "g%d" % (1 + tpar), [
                            lambda g, j=j, blk=blk, cos_t=cos_t: g.dma_start(
                                out=cos_t, in_=cos_d[128 * j:128 * j + 128, 512 * blk:512 * blk + 512]),
                            lambda g, j=j, blk=blk, sin_t=sin_t: g.dma_start(
                                out=sin_t, in_=sin_d[128 * j:128 * j + 128, 512 * blk:512 * blk + 512]),
                        ], gates=gts)
                        ev_c = ev_s2 = ev_tab
                        for (w_sb, bias_col, out_base) in ((wq_sb, bqcol, Q_FM), (wk_sb, bkcol, K_FM)):
                            for (bi, m) in ((0, j), (1, j + 2)):
                                P.guard_write(bi)
                                ev_mm = None
                                for k in range(8):
                                    ev_mm = t_do(lambda t, k=k, m=m, bi=bi, w_sb=w_sb, xln_blk=xln_blk: t.matmul(
                                        out=bank32(bi), lhsT=w_sb[k][:, 128 * m:128 * m + 128],
                                        rhs=xln_blk[k], start=(k == 0), stop=(k == 7)),
                                        inc=(k == 7))
                                P.set_write(bi, ev_mm)
                            Sd.wait_all("v", [ev_c, ev_s2])
                            P.guard_read("v", 0)
                            P.guard_read("v", 1)
                            t1 = m32(SC_T12, 512)
                            t2 = m32(SC_T12 + 512, 512)
                            t3 = m32(SC_T12 + 1024, 512)
                            t4 = m32(SC_T12 + 1536, 512)
                            o1 = mega[:, out_base + 4096 * j + 512 * blk:
                                      out_base + 4096 * j + 512 * blk + 512]
                            o2 = mega[:, out_base + 4096 * (j + 2) + 512 * blk:
                                      out_base + 4096 * (j + 2) + 512 * blk + 512]
                            be = bias_col[:, j:j + 1]
                            bo_ = bias_col[:, j + 2:j + 3]
                            v_do(lambda v, cos_t=cos_t, t1=t1, be=be: v.scalar_tensor_tensor(
                                out=t1, in0=bank32(0), scalar=be, in1=cos_t, op0=ALU.add, op1=ALU.mult))
                            v_do(lambda v, sin_t=sin_t, t2=t2, bo_=bo_: v.scalar_tensor_tensor(
                                out=t2, in0=bank32(1), scalar=bo_, in1=sin_t, op0=ALU.add, op1=ALU.mult))
                            v_do(lambda v, sin_t=sin_t, t3=t3, be=be: v.scalar_tensor_tensor(
                                out=t3, in0=bank32(0), scalar=be, in1=sin_t, op0=ALU.add, op1=ALU.mult))
                            ev_t4 = v_do(lambda v, cos_t=cos_t, t4=t4, bo_=bo_: v.scalar_tensor_tensor(
                                out=t4, in0=bank32(1), scalar=bo_, in1=cos_t, op0=ALU.add, op1=ALU.mult), inc=True)
                            Sd.wait("v", ev_t4)
                            v_do(lambda v, o1=o1, t1=t1, t2=t2: v.tensor_tensor(out=o1, in0=t1, in1=t2, op=ALU.subtract))
                            ev_r2 = v_do(lambda v, o2=o2, t3=t3, t4=t4: v.tensor_tensor(out=o2, in0=t3, in1=t4, op=ALU.subtract), inc=True)
                            P.set_read(0, ev_r2)
                            P.set_read(1, ev_r2)
                            tab_gate[tpar] = ev_r2
                else:
                    Sd.wait_all("t", [evw_q, ev_init])
                    for m in range(4):
                        bi = m % 2
                        P.guard_write(bi)
                        ev_mm = None
                        for k in range(8):
                            ev_mm = t_do(lambda t, k=k, m=m, bi=bi, xln_blk=xln_blk: t.matmul(
                                out=bank32(bi), lhsT=wq_sb[k][:, 128 * m:128 * m + 128],
                                rhs=xln_blk[k], start=(k == 0), stop=(k == 7)), inc=(k == 7))
                        P.set_write(bi, ev_mm)
                        P.guard_read("v", bi)
                        o1 = mega[:, Q_FM + 4096 * m + 512 * blk: Q_FM + 4096 * m + 512 * blk + 512]
                        ev_e = v_do(lambda v, bi=bi, o1=o1: v.tensor_copy(o1, bank32(bi)), inc=True)
                        P.set_read(bi, ev_e)

                if not is_b:
                    Sd.wait("t", evw_v)
                for i in range(4):
                    tt = 4 * blk + i
                    bi = 4 + (i % 2)
                    P.guard_write(bi)
                    for k in range(8):
                        ev_mm = t_do(lambda t, k=k, i=i, bi=bi, wv_sb=wv_sb, xln_blk=xln_blk: t.matmul(
                            out=bank32(bi), lhsT=xln_blk[k][:, 128 * i:128 * i + 128],
                            rhs=wv_sb[k], start=(k == 0), stop=(is_b and k == 7)),
                            inc=(is_b and k == 7))
                    if not is_b:
                        ev_mm = t_do(lambda t, bi=bi: t.matmul(
                            out=bank32(bi), lhsT=e0b[:, :],
                            rhs=mega[:, BV_PAD: BV_PAD + 512],
                            start=False, stop=True), inc=True)
                    P.set_write(bi, ev_mm)
                    P.guard_read("v", bi)
                    Sd.wait_all("v", ev_v65ones)
                    ov = mega[:, R_V65 + 520 * tt: R_V65 + 520 * tt + 520].rearrange(
                        "p (h x) -> p h x", x=65)[:, :, 0:64]
                    ev_e = v_do(lambda v, bi=bi, ov=ov: v.tensor_copy(
                        ov, bank32(bi).rearrange("p (h x) -> p h x", h=8)), inc=True)
                    P.set_read(bi, ev_e)
                blk_gate[par] = tfront()

        def phase_attn(is_b, agin, cc_piece=None):
            phase_gate()
            q_base = Q_FM
            k_base = Q_FM if is_b else K_FM
            ev_stage_dma = {}
            bc_gate = {0: None, 1: None}
            evs_dma = []
            unit = 0
            for c in range(4):
                for qo in range(8):
                    kt_max = 32 if is_b else 4 * (qo + 1)
                    cb0 = 2 * (unit % 2)
                    for b in range(2):
                        P.guard_write(cb0 + b)
                    first = [True, True]
                    pb_ev_ctx = [None, None, None, None]
                    pending = []
                    ev_ctx_last = None

                    def emit_ctx(pend):
                        nonlocal ev_ctx_last
                        kt, c0, pb, ev_exp, ev_mask = pend
                        Sd.wait("t", ev_exp)
                        Sd.wait("t", ev_mask)
                        for h in range(2):
                            st = first[h]
                            first[h] = False
                            ev_ctx_last = t_do(lambda t, kt=kt, c0=c0, pb=pb, h=h, st=st,
                                               sp=(kt == kt_max - 1), c=c, cb0=cb0: t.matmul(
                                out=ps32[0:65, 512 * (cb0 + h) + c0: 512 * (cb0 + h) + 512],
                                lhsT=mega[:, R_V65 + 520 * kt + 65 * (2 * c + h):
                                          R_V65 + 520 * kt + 65 * (2 * c + h) + 65],
                                rhs=pb[:, 512 * h + c0: 512 * h + 512],
                                start=st, stop=sp), inc=(h == 1))
                        pb_ev_ctx[kt % 4] = ev_ctx_last

                    for kt in range(kt_max):
                        c0 = 0 if is_b else max(0, kt * 128 - 512 * qo)
                        pb = m16(P_BUF + 1024 * (kt % 4), 1024)
                        sp2 = kt % 2
                        sreg0 = 2048 + 1024 * sp2
                        ev_sm = None
                        for h in range(2):
                            ol = 64 * h
                            P.guard_write(4 + 2 * sp2 + h)
                            ev_sm = t_do(lambda t, kt=kt, c0=c0, h=h, ol=ol, c=c, qo=qo,
                                         sreg0=sreg0: t.matmul(
                                out=ps32[:, sreg0 + 512 * h + c0: sreg0 + 512 * h + 512],
                                lhsT=mega[ol:ol + 64, k_base + c * 4096 + 128 * kt:
                                          k_base + c * 4096 + 128 * kt + 128],
                                rhs=mega[ol:ol + 64, q_base + c * 4096 + 512 * qo + c0:
                                         q_base + c * 4096 + 512 * qo + 512],
                                start=True, stop=True), inc=(h == 1))
                        for h in range(2):
                            P.set_write(4 + 2 * sp2 + h, ev_sm)
                        Sd.wait("s", pb_ev_ctx[kt % 4])
                        P.guard_read("s", 4 + 2 * sp2)
                        P.guard_read("s", 5 + 2 * sp2)
                        ev_exp = s_do(lambda s, sreg0=sreg0, pb=pb: s.activation(
                            out=pb[:, 0:1024], in_=ps32[:, sreg0: sreg0 + 1024],
                            func=AF.Exp, scale=cA[:, :], bias=cB[:, :]), inc=True)
                        P.set_read(4 + 2 * sp2, ev_exp)
                        P.set_read(5 + 2 * sp2, ev_exp)
                        ev_mask = None
                        if (not is_b) and kt * 128 >= 512 * qo:
                            Sd.wait("v", ev_exp)
                            v_do(lambda v, c0=c0, pb=pb: v.tensor_tensor(
                                out=pb[:, c0:c0 + 128], in0=pb[:, c0:c0 + 128],
                                in1=trib[:, :], op=ALU.mult))
                            ev_mask = v_do(lambda v, c0=c0, pb=pb: v.tensor_tensor(
                                out=pb[:, 512 + c0:512 + c0 + 128],
                                in0=pb[:, 512 + c0:512 + c0 + 128],
                                in1=trib[:, :], op=ALU.mult), inc=True)
                        pending.append((kt, c0, pb, ev_exp, ev_mask))
                        if len(pending) == 3:
                            emit_ctx(pending.pop(0))
                            emit_ctx(pending.pop(0))
                    for pend in pending:
                        emit_ctx(pend)
                    pending = []
                    for b in range(2):
                        P.set_write(cb0 + b, ev_ctx_last)
                    Sd.wait("v", ev_ctx_last)
                    ev_rc = v_do(lambda v, cb0=cb0: v.reciprocal(out=recip_f(),
                                                        in_=ps32[64:65, 512 * cb0: 512 * cb0 + 1024]), inc=True)
                    Sd.wait("v", ev_rc)
                    ev_rcb = v_do(lambda v: v.tensor_copy(
                        mega[0:1, RECIP_PAD: RECIP_PAD + 1024], recip_f()), inc=True)
                    ev_sts = []
                    for h in range(2):
                        bci = 4 + h
                        P.guard_write(bci)
                        Sd.wait("t", ev_rcb)
                        ev_bc = t_do(lambda t, h=h, bci=bci: t.matmul(
                            out=bank32(bci), lhsT=e0b[:, :],
                            rhs=mega[:, RECIP_PAD + 512 * h: RECIP_PAD + 512 * h + 512],
                            start=True, stop=True), inc=True)
                        P.set_write(bci, ev_bc)
                        stg = m16(CTX_STG + 512 * (2 * (unit % 2) + h), 512)
                        Sd.wait("v", ev_stage_dma.get((unit % 2, h)))
                        P.guard_read("v", bci)
                        bc_sb = m32(SC_BC + 512 * h, 512)
                        Sd.wait("v", bc_gate[h])
                        ev_cp = v_do(lambda v, bci=bci, bc_sb=bc_sb: v.tensor_copy(
                            bc_sb[0:64, :], bank32(bci)[0:64, :]), inc=True)
                        P.set_read(bci, ev_cp)
                        Sd.wait("v", ev_cp)
                        ev_st = v_do(lambda v, h=h, stg=stg, bc_sb=bc_sb, cb0=cb0: v.tensor_tensor(
                            out=stg[0:64, :], in0=ps32[0:64, 512 * (cb0 + h): 512 * (cb0 + h) + 512],
                            in1=bc_sb[0:64, :], op=ALU.mult), inc=True)
                        bc_gate[h] = ev_st
                        ev_sts.append(ev_st)
                        ev_d = y_dma(agin[64 * (2 * c + h): 64 * (2 * c + h) + 64,
                                          512 * qo: 512 * qo + 512],
                                     stg[0:64, :], ch="y%d" % (2 + unit % 2),
                                     gates=[ev_st])
                        ev_stage_dma[(unit % 2, h)] = ev_d
                        evs_dma.append(ev_d)
                    for b in range(2):
                        P.set_read(cb0 + b, ev_sts[-1])
                    unit += 1
                if cc_piece is not None:
                    Sd.wait_all("g", evs_dma[-16:])
                    cc_piece(c)
            return evs_dma

        def xres_view():
            return mega[:, R_QK: R_QK + 2 * 16384].bitcast(F32)

        def phase_outproj(agout_h, w_dram, has_bias, ev_cc):
            phase_gate()
            if has_bias:
                Sd.wait("g", tfront())
                Sd.wait("g", sfront())
                Sd.wait("g", vfront())
                ev_z = g_other(lambda g: g.memset(mega[:, BO_PAD: BO_PAD + 1024], 0.0))
                Sd.wait("v", ev_z)
                ev_bp = v_do(lambda v: v.tensor_copy(mega[0:1, BO_PAD: BO_PAD + 1024],
                                                     biaso[0:1, :]), inc=True)
                Sd.wait("t", ev_bp)
            evw = load_w_kxm(w_dram, R_W, 1024, gates=[tfront()])
            wo_sb = [mega[:, R_W + 1024 * k: R_W + 1024 * k + 1024] for k in range(8)]
            ctxf = [mega[:, R_V65 + 2048 * k: R_V65 + 2048 * k + 2048] for k in range(8)]

            def load_ctx_k(g, k):
                row = 256 * (k % 4) + 128 * (k // 4)
                g.reg_alu(REG["tmp"], REG["own"], row * S, ALU.add)
                return g.dma_start(out=ctxf[k],
                                   in_=bass.AP(agout_h, REG["tmp"], [[S, 128], [1, 2048]]))

            ev_ctx = dma_unit("g", "g0",
                              [lambda g, k=k: load_ctx_k(g, k) for k in range(8)],
                              gates=[ev_cc, tfront()])
            if "opj" in dbg:
                Sd.wait("y", ev_ctx)
                barrier_into("y")
                dump("d_ctxf", dbg_dram("d_ctxf", [128, 16384], BF16), m16(R_V65, 16384))
                dump("d_agout", dbg_dram("d_agout", [1024, S], BF16), None,
                     dram_src=agout_h.ap())
            Sd.wait_all("t", [evw, ev_ctx])
            ev_add = None
            for tt in range(16):
                for n2 in range(2):
                    bi = (2 * tt + n2) % 4
                    P.guard_write(bi)
                    for k in range(8):
                        ev_mm = t_do(lambda t, k=k, tt=tt, n2=n2, bi=bi: t.matmul(
                            out=bank32(bi), lhsT=ctxf[k][:, 128 * tt:128 * tt + 128],
                            rhs=wo_sb[k][:, 512 * n2:512 * n2 + 512],
                            start=(k == 0), stop=(not has_bias and k == 7)),
                            inc=(not has_bias and k == 7))
                    if has_bias:
                        ev_mm = t_do(lambda t, n2=n2, bi=bi: t.matmul(
                            out=bank32(bi), lhsT=e0b[:, :],
                            rhs=mega[:, BO_PAD + 512 * n2: BO_PAD + 512 * n2 + 1024 - 512],
                            start=False, stop=True), inc=True)
                    P.set_write(bi, ev_mm)
                    P.guard_read("v", bi)
                    if "ydump" in dbg and tt == 0 and n2 == 0:
                        yscr = m16(P_BUF, 1024).bitcast(F32)
                        ev_yd = v_do(lambda v, bi=bi, yscr=yscr: v.tensor_copy(
                            yscr, bank32(bi)), inc=True)
                        Sd.wait("y", ev_yd)
                        dump("d_y0", dbg_dram("d_y0", [128, 512], F32), yscr)
                    xr = xres_view()
                    ev_add = v_do(lambda v, tt=tt, n2=n2, bi=bi, xr=xr: v.tensor_tensor(
                        out=xr[:, 1024 * tt + 512 * n2: 1024 * tt + 512 * n2 + 512],
                        in0=xr[:, 1024 * tt + 512 * n2: 1024 * tt + 512 * n2 + 512],
                        in1=bank32(bi), op=ALU.add), inc=True)
                    P.set_read(bi, ev_add)
            return ev_add


        def phase_ffn(half_done=None):
            phase_gate()
            xr = xres_view()
            ev_wd = g_dma(mega[:, R_WD: R_WD + 24576].rearrange("p (k m) -> p k m", k=24),
                          wdT_d.rearrange("(k p) m -> p k m", p=128),
                          gates=[tfront(), vfront()])
            wd_sb = [mega[:, R_WD + 1024 * hc: R_WD + 1024 * hc + 1024] for hc in range(24)]
            ev_r = None
            for half in range(2):
                evs = []
                for i in range(8):
                    tt = 8 * half + i
                    ov = mega[:, R_XLN: R_XLN + 8192].rearrange(
                        "p (c x) -> p c x", x=1024)[:, :, 128 * i:128 * i + 128]
                    evs.append(ln_core(xr[:, 1024 * tt: 1024 * tt + 1024], tt, ov,
                                       stage_base=R_W + 5120, junk_base=R_W + 7168)[0])
                xlnh = [mega[:, R_XLN + 1024 * k: R_XLN + 1024 * k + 1024] for k in range(8)]
                Sd.wait("t", evs[-1])
                hv_evs = {0: vfront(), 1: vfront()}
                gs_gate = {0: vfront(), 1: vfront()}
                ev_h = None
                for hc in range(24):
                    par = hc % 2
                    wbase = R_W + 2048 * par
                    wgv = mega[:, wbase: wbase + 1024].rearrange("p (k m) -> p k m", k=8)
                    wuv = mega[:, wbase + 1024: wbase + 2048].rearrange("p (k m) -> p k m", k=8)
                    gts = [hv_evs[par]] if hv_evs[par] else []
                    ev_wu2 = dma_unit("g", "g%d" % (1 + par), [
                        lambda g, hc=hc, wgv=wgv: g.dma_start(
                            out=wgv, in_=wgT_d[:, 128 * hc:128 * hc + 128].rearrange(
                                "(k p) m -> p k m", p=128)),
                        lambda g, hc=hc, wuv=wuv: g.dma_start(
                            out=wuv, in_=wuT_d[:, 128 * hc:128 * hc + 128].rearrange(
                                "(k p) m -> p k m", p=128)),
                    ], gates=gts)
                    ev_g1 = ev_g2 = ev_wu2
                    wg_k = [mega[:, wbase + 128 * k: wbase + 128 * k + 128] for k in range(8)]
                    wu_k = [mega[:, wbase + 1024 + 128 * k: wbase + 1024 + 128 * k + 128] for k in range(8)]
                    Sd.wait_all("t", [ev_g1, ev_g2])
                    for tb in range(2):
                        big = (2 * tb) % 4
                        biu = big + 1
                        P.guard_write(big)
                        ev_gm = None
                        for k in range(8):
                            ev_gm = t_do(lambda t, k=k, tb=tb, big=big, wg_k=wg_k, xlnh=xlnh: t.matmul(
                                out=bank32(big), lhsT=wg_k[k],
                                rhs=xlnh[k][:, 512 * tb:512 * tb + 512],
                                start=(k == 0), stop=(k == 7)), inc=(k == 7))
                        P.set_write(big, ev_gm)
                        P.guard_write(biu)
                        ev_um = None
                        for k in range(8):
                            ev_um = t_do(lambda t, k=k, tb=tb, biu=biu, wu_k=wu_k, xlnh=xlnh: t.matmul(
                                out=bank32(biu), lhsT=wu_k[k],
                                rhs=xlnh[k][:, 512 * tb:512 * tb + 512],
                                start=(k == 0), stop=(k == 7)), inc=(k == 7))
                        P.set_write(biu, ev_um)
                        P.guard_read("s", big)
                        gpar = (2 * hc + tb) % 2
                        gs = m16(R_W + 4096 + 512 * gpar, 512)
                        Sd.wait("s", gs_gate[gpar])
                        ev_si = s_do(lambda s, big=big, gs=gs: s.activation(
                            out=gs, in_=bank32(big), func=AF.Silu, scale=cC[:, :]), inc=True)
                        P.set_read(big, ev_si)
                        hv = mega[:, R_V65 + 1024 * hc + 512 * tb:
                                  R_V65 + 1024 * hc + 512 * tb + 512]
                        Sd.wait("v", ev_si)
                        P.guard_read("v", biu)
                        ev_h = v_do(lambda v, biu=biu, gs=gs, hv=hv: v.tensor_tensor(
                            out=hv, in0=gs, in1=bank32(biu), op=ALU.mult), inc=True)
                        P.set_read(biu, ev_h)
                        gs_gate[gpar] = ev_h
                    hv_evs[par] = ev_h
                Sd.wait("t", ev_wd)
                Sd.wait("t", ev_h)
                for i in range(8):
                    tt = 8 * half + i
                    for n2 in range(2):
                        bi = (2 * i + n2) % 4
                        P.guard_write(bi)
                        ev_dm = None
                        for hc in range(24):
                            ev_dm = t_do(lambda t, hc=hc, i=i, n2=n2, bi=bi: t.matmul(
                                out=bank32(bi),
                                lhsT=mega[:, R_V65 + 1024 * hc + 128 * i:
                                          R_V65 + 1024 * hc + 128 * i + 128],
                                rhs=wd_sb[hc][:, 512 * n2:512 * n2 + 512],
                                start=(hc == 0), stop=(hc == 23)), inc=(hc == 23))
                        P.set_write(bi, ev_dm)
                        P.guard_read("v", bi)
                        ev_r = v_do(lambda v, tt=tt, n2=n2, bi=bi, xr=xr: v.scalar_tensor_tensor(
                            out=xr[:, 1024 * tt + 512 * n2: 1024 * tt + 512 * n2 + 512],
                            in0=bank32(bi), scalar=1.0 / (QS * QS),
                            in1=xr[:, 1024 * tt + 512 * n2: 1024 * tt + 512 * n2 + 512],
                            op0=ALU.mult, op1=ALU.add), inc=True)
                        P.set_read(bi, ev_r)
                if half_done is not None:
                    half_done(half, ev_r)
            return ev_r

        def dump(name, dram_ap, sb_view, dram_src=None):
            if dram_src is not None:
                ev = g_dma(dram_ap, dram_src)
                Sd.wait("g", ev)
            else:
                ev = y_dma(dram_ap, sb_view)
                Sd.wait("y", ev)

        def barrier_into(eng):
            for sem, c in Sd.counts.items():
                Sd.wait(eng, (sem, c))

        def main_tail(cc_evs_x):
            xr = xres_view()
            Sd.wait_all("y", cc_evs_x)
            phase_proj(is_b=True)
            if "stop_qkvb" in dbg:
                barrier_into("y")
                dump("d_qkvb", dbg_dram("d_qkvb", [512, S], BF16).rearrange(
                    "(a p) c -> p a c", p=128),
                    m16(Q_FM, 16384).rearrange("p (a c) -> p a c", a=4))
                return
            cc_evs_b = []

            def cc_piece_b(c):
                ev = Sd.do("g", lambda g, c=c: g.collective_compute(
                    "AllGather", ALU.bypass, replica_groups=groups,
                    ins=[agin_b[128 * c:128 * c + 128, :].opt()],
                    outs=[agout_b[256 * c:256 * c + 256, :].opt()]), "cc", 1)
                cc_evs_b.append(ev)

            phase_attn(is_b=True, agin=agin_b, cc_piece=cc_piece_b)
            ev_cc_b = cc_evs_b[-1]
            ev_xr2 = y_dma(xr.rearrange("p (a c) -> p a c", a=16),
                           agin_x.rearrange("(a p) c -> p a c", p=128),
                           gates=[tfront()])
            Sd.wait("v", ev_xr2)
            phase_outproj(agout_b_h, Wob_d, has_bias=False, ev_cc=ev_cc_b)
            if "x3" in dbg:
                barrier_into("y")
                dump("d_x3", dbg_dram("d_x3", [T_OWN, D], F32).rearrange("(a p) c -> p a c", p=128),
                     xr.rearrange("p (a c) -> p a c", a=16))
            if "stop_x3" in dbg:
                return
            phase_ffn()
            ev_out = y_dma(y_out.rearrange("(a p) c -> p a c", p=128),
                           xr.rearrange("p (a c) -> p a c", a=16),
                           gates=[vfront()])
            Sd.wait("y", ev_out)

        phase_proj(is_b=False)
        if "qkv" in dbg:
            barrier_into("y")
            dump("d_q", dbg_dram("d_q", [512, S], BF16).rearrange("(a p) c -> p a c", p=128),
                 m16(Q_FM, 16384).rearrange("p (a c) -> p a c", a=4))
            dump("d_k", dbg_dram("d_k", [512, S], BF16).rearrange("(a p) c -> p a c", p=128),
                 m16(K_FM, 16384).rearrange("p (a c) -> p a c", a=4))
            dump("d_v", dbg_dram("d_v", [128, 32 * 520], BF16),
                 m16(R_V65, 16640))
        if "stop_qkv" in dbg:
            barrier_into("y")
        elif "stop_ctxa" in dbg:
            evs_a = phase_attn(is_b=False, agin=agin_a)
            Sd.wait_all("g", evs_a)
            barrier_into("y")
            dump("d_ctxa", dbg_dram("d_ctxa", [512, S], BF16), None, dram_src=agin_a)
        else:
            cc_evs_a = []

            def cc_piece_a(c):
                ev = Sd.do("g", lambda g, c=c: g.collective_compute(
                    "AllGather", ALU.bypass, replica_groups=groups,
                    ins=[agin_a[128 * c:128 * c + 128, :].opt()],
                    outs=[agout_a[256 * c:256 * c + 256, :].opt()]), "cc", 1)
                cc_evs_a.append(ev)

            phase_attn(is_b=False, agin=agin_a, cc_piece=cc_piece_a)
            ev_cc_a = cc_evs_a[-1]
            xr = xres_view()
            ev_xr = y_dma(xr.rearrange("p (a c) -> p a c", a=16),
                          xown.rearrange("(a p) c -> p a c", p=128),
                          gates=[tfront()])
            Sd.wait("v", ev_xr)
            phase_outproj(agout_a_h, Woa_d, has_bias=True, ev_cc=ev_cc_a)
            if "x1" in dbg:
                barrier_into("y")
                dump("d_x1", dbg_dram("d_x1", [T_OWN, D], F32).rearrange("(a p) c -> p a c", p=128),
                     xr.rearrange("p (a c) -> p a c", a=16))
            if "stop_x1" in dbg:
                return_early = True
            else:
                return_early = False
            cc_evs_x = []

            def x_half_done(half, ev_r):
                xv = xres_view()[:, 8192 * half: 8192 * half + 8192].rearrange(
                    "p (a c) -> p a c", a=8)
                g_dma(agin_x[1024 * half:1024 * half + 1024, :].rearrange(
                    "(a p) c -> p a c", p=128), xv, gates=[ev_r])
                ev_w = g_dma(agin_xb[1024 * half:1024 * half + 1024, :].rearrange(
                    "(a p) c -> p a c", p=128), xv)
                Sd.wait("g", ev_w)
                ev = Sd.do("g", lambda g, half=half: g.collective_compute(
                    "AllGather", ALU.bypass, replica_groups=groups,
                    ins=[agin_xb[1024 * half:1024 * half + 1024, :].opt()],
                    outs=[agout_xb[2048 * half:2048 * half + 2048, :].opt()]), "cc", 1)
                cc_evs_x.append(ev)

            if not return_early:
                phase_ffn(half_done=x_half_done)
                if "x2" in dbg:
                    barrier_into("y")
                    dump("d_x2", dbg_dram("d_x2", [T_OWN, D], F32).rearrange("(a p) c -> p a c", p=128),
                         xr.rearrange("p (a c) -> p a c", a=16))
            if return_early or "stop_x2" in dbg:
                pass
            else:
                main_tail(cc_evs_x)
        if False:
            pass

        with nc.Block() as block:
            @block.tensor
            def _(t):
                Sd.replay("t", t, sems)

            @block.vector
            def _(v):
                Sd.replay("v", v, sems)

            @block.scalar
            def _(s):
                Sd.replay("s", s, sems)

            @block.gpsimd
            def _(g):
                Sd.replay("g", g, sems)

            @block.sync
            def _(y):
                Sd.replay("y", y, sems)

        ninstr = {e: len(Sd.cmds[e]) for e in Sd.ENGS}
    return nc, ninstr


def host_prep(inputs):
    Wq, bq = np.asarray(inputs["L0_attn_a_Wq"]), np.asarray(inputs["L0_attn_a_bq"])
    Wk, bk = np.asarray(inputs["L0_attn_a_Wk"]), np.asarray(inputs["L0_attn_a_bk"])
    Wv, bv = np.asarray(inputs["L0_attn_a_Wv"]), np.asarray(inputs["L0_attn_a_bv"])
    Wo, bo = np.asarray(inputs["L0_attn_a_Wo"]), np.asarray(inputs["L0_attn_a_bo"])
    Wqkv = np.asarray(inputs["L0_attn_b_W_qkv"])
    Wob = np.asarray(inputs["L0_attn_b_W_o"])
    wg = np.asarray(inputs["L0_ffn_w_gate"])
    wu = np.asarray(inputs["L0_ffn_w_up"])
    wd = np.asarray(inputs["L0_ffn_w_down"])
    x0 = np.asarray(inputs["L0_attn_a_x"])

    j = np.arange(512, dtype=np.float32)
    freq_cis = np.power(np.float32(10000.0), np.float32(-2.0) * j / np.float32(D))
    pos = np.arange(S, dtype=np.float32)
    freq = np.outer(pos, freq_cis).astype(np.float32)
    cosf = np.cos(freq).astype(np.float32)
    sinf = np.sin(freq).astype(np.float32)

    perm = np.empty(1024, dtype=np.int64)
    for pp in (0, 1):
        for d_ in range(512):
            r = 512 * pp + d_
            perm[r] = (256 * pp + d_) if d_ < 256 else (512 + 256 * pp + (d_ - 256))
    Wo_perm = np.ascontiguousarray(Wo[perm, :])

    tri = np.triu(np.ones((128, 128), np.float32)).astype(ml_dtypes.bfloat16)
    ident = np.eye(128, dtype=ml_dtypes.bfloat16)
    e0 = np.zeros((128, 128), dtype=ml_dtypes.bfloat16)
    e0[0, :] = 1
    wgT = np.ascontiguousarray(wg.T)
    wuT = np.ascontiguousarray(wu.T)
    wdT = np.ascontiguousarray(wd.T)
    WobT = np.ascontiguousarray(Wob.T)

    per_core = []
    for core in range(8):
        s, p = core // 2, core % 2
        sl = slice(512 * p, 512 * p + 512)
        Wq_c = np.ascontiguousarray(np.concatenate(
            [Wq[:, sl][:, 0::2], Wq[:, sl][:, 1::2]], axis=1))
        Wk_c = np.ascontiguousarray(np.concatenate(
            [Wk[:, sl][:, 0::2], Wk[:, sl][:, 1::2]], axis=1))
        bq_cc = np.concatenate([bq[sl][0::2], bq[sl][1::2]]).astype(np.float32)
        bk_cc = np.concatenate([bk[sl][0::2], bk[sl][1::2]]).astype(np.float32)
        bqcol = np.ascontiguousarray(bq_cc.reshape(4, 128).T)
        bkcol = np.ascontiguousarray(bk_cc.reshape(4, 128).T)
        gheads = [4 * p + l for l in range(4)] + [8 + 4 * p + l for l in range(4)]
        vcols = np.concatenate([np.arange(64 * g, 64 * g + 64) for g in gheads])
        Wv_c = np.ascontiguousarray(Wv[:, vcols])
        bv_c = np.ascontiguousarray(bv[vcols])[None, :]
        jsl = slice(256 * p, 256 * p + 256)
        cos_c = np.ascontiguousarray(cosf[:, jsl].T)
        sin_c = np.ascontiguousarray(sinf[:, jsl].T)
        Wqkv_c = np.ascontiguousarray(Wqkv[512 * p: 512 * p + 512, :].T)
        per_core.append({
            "xseq": np.ascontiguousarray(x0[s]),
            "xown": np.ascontiguousarray(x0[s][2048 * p: 2048 * p + 2048]),
            "Wq_c": Wq_c, "Wk_c": Wk_c, "Wv_c": Wv_c,
            "bqcol": bqcol, "bkcol": bkcol,
            "bv_c": np.ascontiguousarray(bv_c),
            "e0": e0,
            "Wo_a": Wo_perm, "bo_a": np.ascontiguousarray(bo[None, :]),
            "cos_c": cos_c, "sin_c": sin_c,
            "Wqkv_c": Wqkv_c, "Wo_b": WobT,
            "wgT": wgT, "wuT": wuT, "wdT": wdT,
            "tri": tri, "ident": ident,
        })
    return per_core


def kernel(**inputs):
    from concourse.bass_utils import run_bass_kernel_spmd
    per_core = host_prep(inputs)
    nc, _ = build()
    res = run_bass_kernel_spmd(nc, per_core, core_ids=list(range(8)))
    out = np.zeros((B, S, D), np.float32)
    for core in range(8):
        s, p = core // 2, core % 2
        out[s, 2048 * p: 2048 * p + 2048] = res.results[core]["y_out"]
    return out
